# version 23
# baseline (speedup 1.0000x reference)
import numpy as np
import ml_dtypes
import concourse.bass as bass
import concourse.mybir as mybir
from concourse.bass_utils import run_bass_kernel_spmd
from contextlib import ExitStack

F32 = mybir.dt.float32
BF16 = mybir.dt.bfloat16
AF = mybir.ActivationFunctionType
ALU = mybir.AluOpType
AX = mybir.AxisListType

D = 2048
DL = 512
DC = 256
LN_EPS = 1e-5
SCALE = 0.0625


def set_dims(t=4096, c=1024, hid=8192, hg=1024):
    global T, NT, NG, C, NSW, HID, HG, A_BLK, NBT, FWHT_SCALE
    T = t
    NT = T // 128
    NG = NT // 4
    C = c
    NSW = T // C
    HID = hid
    HG = hg
    A_BLK = HID // 128
    NBT = 128 // A_BLK
    FWHT_SCALE = float(np.float32(1.0 / np.sqrt(np.float32(HID))))


set_dims()

ENGINES = ("sync", "scalar", "vector", "tensor", "gpsimd")


def _sylvester(n):
    h = np.array([[1.0]], dtype=np.float64)
    while h.shape[0] < n:
        h = np.block([[h, h], [h, -h]])
    return h


def host_consts():
    inv = (1.0 / (10000.0 ** (np.arange(0, DC, 2).astype(np.float32) / np.float32(DC)))).astype(np.float32)
    t = np.arange(T, dtype=np.float32)
    freqs = inv[:, None] * t[None, :]
    cosT = np.cos(freqs).astype(np.float32)
    sinT = np.sin(freqs).astype(np.float32)
    return cosT, sinT


class PB:

    NDMA = 6

    def __init__(self, nc):
        self.nc = nc
        self.ops = {e: [] for e in ENGINES}
        self.cnt = {}
        self.sems = {}
        self.wmax = {e: {} for e in ENGINES}
        self.rr = {}

    def sem(self, name):
        self.sems[name] = self.nc.alloc_semaphore(name)
        self.cnt[name] = 0

    def dma_pool(self, queue, engine=None, n=None):
        names = [f"d{queue[0].upper()}{queue[-1]}{i}" for i in range(n or self.NDMA)]
        for nm in names:
            if nm not in self.sems:
                self.sem(nm)
        self.rr[queue] = [names, 0, engine or queue]

    def dma(self, queue, out_ap, in_ap, waits=(), transpose=False):
        names, idx, engine = self.rr[queue]
        s = names[idx % len(names)]
        self.rr[queue][1] = idx + 1
        prev = self.cnt[s]
        if transpose:
            fn = lambda e, o=out_ap, i=in_ap: e.dma_start(out=o, in_=i, transpose=True)
        else:
            fn = lambda e, o=out_ap, i=in_ap: e.dma_start(out=o, in_=i)
        v = self.emit(engine, fn, waits=list(waits) + [(s, prev)], inc=(s, 16))
        return (s, v)

    def dma_all_done(self, queue):
        names = self.rr[queue][0]
        return [(n, self.cnt[n]) for n in names if self.cnt[n]]

    def emit(self, eng, fn, waits=(), inc=None):
        w = {}
        wm = self.wmax[eng]
        for s, v in waits:
            if v and v > wm.get(s, 0) and v > w.get(s, 0):
                w[s] = v
        for s, v in w.items():
            wm[s] = v
        incv = None
        ret = None
        if inc:
            s, by = inc
            self.cnt[s] += by
            incv = (s, self.cnt[s], by)
            ret = self.cnt[s]
        sems = self.sems

        def op(e, fn=fn, w=tuple(w.items()), incv=incv):
            for s, v in w:
                e.wait_ge(sems[s], v)
            r = fn(e) if fn is not None else None
            if incv is not None:
                assert r is not None, "inc requested but fn returned no instruction"
                r.then_inc(sems[incv[0]], incv[2])

        self.ops[eng].append(op)
        return ret


class Trk:

    def __init__(self):
        self.w = {}
        self.r = {}

    def wrote(self, sem, val):
        self.w[sem] = max(self.w.get(sem, 0), val)

    def read(self, sem, val):
        self.r[sem] = max(self.r.get(sem, 0), val)

    def raw(self):
        return list(self.w.items())

    def war(self):
        m = dict(self.w)
        for s, v in self.r.items():
            m[s] = max(m.get(s, 0), v)
        return list(m.items())


def build(phase_a=True, phase_b=True, debug=False):
    nc = bass.Bass("TRN2", target_bir_lowering=False)

    def din(name, shape, dt=F32):
        return nc.dram_tensor(name, shape, dt, kind="ExternalInput").ap()

    x_d = din("x", [T, D])
    wtkv_d = din("wtkv", [DL, D])
    wc_d = din("wc", [DC, DL])
    gam_d = din("gam", [DC])
    bet_d = din("bet", [DC])
    wq_d = din("wq", [DC, DC])
    wkv_d = din("wkv", [2 * DC, DC])
    wc2l_d = din("wc2l", [DL, DC])
    wor_d = din("wor", [D, DL])
    bor_d = din("bor", [D])
    w1_d = din("w1", [HID, D])
    w2_d = din("w2", [D, HID])
    w3_d = din("w3", [HID, D])
    cosT_d = din("cosT", [128, T])
    sinT_d = din("sinT", [128, T])

    out_d = nc.dram_tensor("out", [T, D], F32, kind="ExternalOutput").ap()

    def scratch(name, shape, dt, dbg=False):
        kind = "ExternalOutput" if (debug and dbg) else "Internal"
        return nc.dram_tensor(name, shape, dt, kind=kind).ap()

    wtkvn = scratch("wtkvn", [DL, D], BF16)
    wcn = scratch("wcn", [DC, DL], BF16)
    wqn = scratch("wqn", [DC, DC], BF16)
    wkvn = scratch("wkvn", [2 * DC, DC], BF16)
    wc2ln = scratch("wc2ln", [DL, DC], BF16)
    worn = scratch("worn", [D, DL], BF16)
    w1n = scratch("w1n", [HID, D], BF16)
    w3n = scratch("w3n", [HID, D], BF16)
    w2n = scratch("w2n", [D, HID], BF16)
    w1t = scratch("w1t", [D, HID], BF16)
    w3t = scratch("w3t", [D, HID], BF16)
    w2s1 = scratch("w2s1", [HID, D], BF16)
    w2p = scratch("w2p", [HID, D], BF16, dbg=True)
    xmid_d = scratch("xmid", [T, D], F32, dbg=True)
    xln2_d = scratch("xln2", [T, D], BF16, dbg=True)
    dbgXT = scratch("dbgXT", [D, 1024], BF16, dbg=True)
    dbgW1 = scratch("dbgW1", [128, 16 * 256], BF16, dbg=True)
    dbgH = scratch("dbgH", [1024, 1024], BF16, dbg=True)
    dbgO = scratch("dbgO", [1024, D], F32, dbg=True)

    ident_i = nc.inline_tensor(np.eye(128, dtype=ml_dtypes.bfloat16), "ident").ap()
    h128_i = nc.inline_tensor(_sylvester(128).astype(ml_dtypes.bfloat16), "h128").ap()
    hbd_np = np.kron(np.eye(NBT), _sylvester(A_BLK))
    hbd_i = nc.inline_tensor(hbd_np.astype(ml_dtypes.bfloat16), "hbd").ap()
    qi, ki = np.meshgrid(np.arange(128), np.arange(128), indexing="ij")
    mask_np = np.where(ki <= qi, np.float32(0.0), np.float32(-1e30)).astype(np.float32)
    mask_i = nc.inline_tensor(mask_np.T.copy().astype(ml_dtypes.bfloat16), "mask").ap()
    ones_i = nc.inline_tensor(np.ones((1, 128), dtype=ml_dtypes.bfloat16), "ones1").ap()

    p = PB(nc)
    for s in ("sT", "sV", "sC", "sP"):
        p.sem(s)
    p.dma_pool("sync")
    p.dma_pool("gpsimd")
    p.dma_pool("s2", engine="sync", n=10)
    p.dma_pool("vdma", engine="gpsimd", n=8)

    st_A = ExitStack()

    def sb(name, shape, dt):
        return st_A.enter_context(nc.sbuf_tensor(name, shape, dt))

    cosg_s = sb("cosg_s", [128, 512], F32)
    sing_s = sb("sing_s", [128, 512], F32)
    ident_s = sb("ident_s", [128, 128], BF16)
    h128_s = sb("h128_s", [128, 128], BF16)
    hbd_s = sb("hbd_s", [128, 128], BF16)
    mask_s = sb("mask_s", [128, 128], BF16)
    wtkvT_s = sb("wtkvT_s", [128, 16 * DL], BF16)
    wcT_s = sb("wcT_s", [128, 4 * DC], BF16)
    wqT_s = sb("wqT_s", [128, 2 * DC], BF16)
    wkvT_s = sb("wkvT_s", [128, 2 * 2 * DC], BF16)
    wc2lT_s = sb("wc2lT_s", [128, 2 * DL], BF16)
    wcomb_s = sb("wcomb_s", [128, 4 * D], BF16)
    gam_s = sb("gam_s", [128, 2], F32)
    bet_s = sb("bet_s", [128, 2], F32)
    betbf_s = sb("betbf_s", [128, 2], BF16)
    bq_s = sb("bq_s", [128, 2], F32)
    bkv_s = sb("bkv_s", [128, 4], F32)
    KT_s = sb("KT_s", [128, 2 * T], BF16)
    VST = DC + 16
    V_s = sb("V_s", [128, NT * VST], BF16)
    xa_b = [sb(f"xa{i}", [128, D], F32) for i in range(2)]
    xlnbf_s = sb("xlnbf_s", [128, D], BF16)
    xlnT_s = sb("xlnT_s", [128, 16 * 512], BF16)
    kvlT_s = sb("kvlT_s", [128, 4 * 512], BF16)
    c_f32 = sb("c_f32", [128, DC], F32)
    cnbf_s = sb("cnbf_s", [128, DC], BF16)
    cnT_s = sb("cnT_s", [128, 2 * 512], BF16)
    qb_s = sb("qb_s", [128, 2 * 512], BF16)
    ropet0 = sb("ropet0", [128, 512], F32)
    ropet1 = sb("ropet1", [128, 512], F32)
    qT_s = sb("qT_s", [128, 2 * 512], BF16)
    vbf_s = sb("vbf_s", [128, 512], BF16)
    o_s = sb("o_s", [128, 4 * DC], BF16)
    oT_s = sb("oT_s", [128, 2 * 512], BF16)
    zT_s = sb("zT_s", [128, 4 * 512], BF16)
    xmid_s = sb("xmid_s", [128, D], F32)
    stat_s = sb("stat_s", [128, 64], F32)
    strip_b = [sb(f"strip{i}", [128, 2048], BF16) for i in range(2)]
    s1i = [sb(f"s1i{i}", [128, D], BF16) for i in range(2)]
    s1o = [sb(f"s1o{i}", [128, D], BF16) for i in range(2)]
    onesb_s = sb("onesb_s", [1, 128], BF16)
    borrow_s = sb("borrow_s", [1, D], F32)
    borbf_s = sb("borbf_s", [1, D], BF16)

    st_psA = ExitStack()
    p_bf = st_psA.enter_context(nc.psum_tensor("p_bf", [128, 1024], BF16))
    pS0 = st_psA.enter_context(nc.psum_tensor("pS0", [128, 512], F32))
    pS1 = st_psA.enter_context(nc.psum_tensor("pS1", [128, 512], F32))
    pX = st_psA.enter_context(nc.psum_tensor("pX", [128, 512], F32))
    pO = st_psA.enter_context(nc.psum_tensor("pO", [128, 2048], F32))

    tk = {}
    for nm in ("xa0", "xa1", "xlnbf", "xlnT", "kvlT", "c_f32", "cnbf", "cnT", "qb", "rt0",
               "rt1", "qT", "vbf", "o", "oT", "zT", "xmid", "scores",
               "stat", "pS0", "pS1", "pX",
               "strip0", "strip1", "s1i0", "s1i1", "s1o0", "s1o1", "wcomb", "wqT", "wkvT",
               "betbf", "borbf", "KT", "V", "xln2T", "w1a", "w1b", "w3a", "w3b",
               "w2g", "hid", "oacc", "sg", "pG", "pL", "pDa", "pDb", "xbm", "xbm2", "outt",
               "cosg", "sing", "expb0", "expb1"):
        tk[nm] = Trk()
    pbf_slot = [Trk() for _ in range(8)]
    tkpo = [Trk() for _ in range(4)]
    st_trks = {}

    def stt(scol):
        return st_trks.setdefault(scol, Trk())

    def po_war():
        m = {}
        for trk in tkpo:
            for s, v in trk.war():
                m[s] = max(m.get(s, 0), v)
        return list(m.items())

    E = p.emit

    casts = [(wtkvn, wtkv_d), (wcn, wc_d), (wqn, wq_d), (wkvn, wkv_d),
             (wc2ln, wc2l_d), (worn, wor_d)]
    cast_done = {}
    for dst, srca in casts:
        cast_done[dst.tensor.name] = p.dma("gpsimd", dst[:, :], srca[:, :])

    def ld(dst_ap, src_ap, waits=()):
        return p.dma("sync", dst_ap, src_ap, waits=waits)

    ld(ident_s[:, :], ident_i[:, :])
    ld(h128_s[:, :], h128_i[:, :])
    ld(hbd_s[:, :], hbd_i[:, :])
    ld(mask_s[:, :], mask_i[:, :])
    gam2 = gam_d.rearrange("(a b) -> a b", b=1)
    bet2 = bet_d.rearrange("(a b) -> a b", b=1)
    v_gam0 = ld(gam_s[:, 0:1], gam2[0:128])
    ld(gam_s[:, 1:2], gam2[128:256])
    ld(bet_s[:, 0:1], bet2[0:128])
    v_bet = ld(bet_s[:, 1:2], bet2[128:256])

    def ldt(dst_ap, src_ap, waits=()):
        return p.dma("sync", dst_ap, src_ap, waits=waits, transpose=True)

    for dc in range(16):
        v = ldt(wtkvT_s[:, dc * DL:(dc + 1) * DL], wtkvn[0:DL, dc * 128:(dc + 1) * 128],
                waits=[cast_done["wtkvn"]])
    wtkvT_done = v
    for lc in range(4):
        v = ldt(wcT_s[:, lc * DC:(lc + 1) * DC], wcn[0:DC, lc * 128:(lc + 1) * 128],
                waits=[cast_done["wcn"]])
    wcT_done = v
    for cc in range(2):
        v = ldt(wqT_s[:, cc * DC:(cc + 1) * DC], wqn[0:DC, cc * 128:(cc + 1) * 128],
                waits=[cast_done["wqn"]])
    wqT_ld = v
    for cc in range(2):
        v = ldt(wkvT_s[:, cc * 2 * DC:(cc + 1) * 2 * DC], wkvn[0:2 * DC, cc * 128:(cc + 1) * 128],
                waits=[cast_done["wkvn"]])
    wkvT_ld = v
    for cc in range(2):
        v = ldt(wc2lT_s[:, cc * DL:(cc + 1) * DL], wc2ln[0:DL, cc * 128:(cc + 1) * 128],
                waits=[cast_done["wc2ln"]])
    wc2lT_done = v

    wcomb_done = 0
    for lc in range(4):
        xb = xa_b[lc % 2]
        xbn = f"xa{lc % 2}"
        v1 = ldt(wcomb_s[:, lc * D:(lc + 1) * D], worn[0:D, lc * 128:(lc + 1) * 128],
                 waits=[cast_done["worn"]] + tk["wcomb"].war())
        v2 = ld(xb[:, :], wtkv_d[lc * 128:(lc + 1) * 128, :], waits=tk[xbn].war())
        tk[xbn].wrote(*v2)
        vv = E("vector", lambda e, lc=lc, xb=xb: e.tensor_tensor(
            out=wcomb_s[:, lc * D:(lc + 1) * D], in0=wcomb_s[:, lc * D:(lc + 1) * D],
            in1=xb[:, :], op=ALU.add),
            waits=[v1, v2], inc=("sV", 1))
        tk[xbn].read("sV", vv)
        tk["wcomb"].wrote("sV", vv)
        wcomb_done = vv

    v = E("vector", lambda e: e.tensor_copy(betbf_s[:, :], bet_s[:, :]),
          waits=[v_bet], inc=("sV", 1))
    tk["betbf"].wrote("sV", v)

    BE = [None]

    def bias_prep():
        for qc in range(2):
            for cc in range(2):
                v = E("tensor", lambda e, qc=qc, cc=cc: e.matmul(
                    pX[:, qc:qc + 1], wqT_s[:, cc * DC + qc * 128: cc * DC + qc * 128 + 128],
                    betbf_s[:, cc:cc + 1], start=(cc == 0), stop=(cc == 1)),
                    waits=[wqT_ld] + tk["betbf"].raw() + tk["pX"].war(),
                    inc=("sT", 1))
        tk["pX"].wrote("sT", v)
        bq_mm = v
        for kc in range(4):
            for cc in range(2):
                v = E("tensor", lambda e, kc=kc, cc=cc: e.matmul(
                    pX[:, 4 + kc: 5 + kc], wkvT_s[:, cc * 2 * DC + kc * 128: cc * 2 * DC + kc * 128 + 128],
                    betbf_s[:, cc:cc + 1], start=(cc == 0), stop=(cc == 1)),
                    waits=[wkvT_ld] + tk["betbf"].raw(),
                    inc=("sT", 1))
        tk["pX"].wrote("sT", v)
        bkv_mm = v
        v = E("scalar", lambda e: e.copy(bq_s[:, 0:2], pX[:, 0:2]),
              waits=[("sT", bq_mm)], inc=("sC", 1))
        v = E("scalar", lambda e: e.copy(bkv_s[:, 0:4], pX[:, 4:8]),
              waits=[("sT", bkv_mm)], inc=("sC", 1))
        tk["pX"].read("sC", v)
        BE[0] = v
        for cc in range(2):
            v = E("vector", lambda e, cc=cc: e.tensor_scalar(
                out=wqT_s[:, cc * DC:(cc + 1) * DC], in0=wqT_s[:, cc * DC:(cc + 1) * DC],
                scalar1=gam_s[:, cc:cc + 1], scalar2=None, op0=ALU.mult),
                waits=[("sT", bq_mm), wqT_ld], inc=("sV", 1))
        tk["wqT"].wrote("sV", v)
        for cc in range(2):
            v = E("vector", lambda e, cc=cc: e.tensor_scalar(
                out=wkvT_s[:, cc * 2 * DC:(cc + 1) * 2 * DC], in0=wkvT_s[:, cc * 2 * DC:(cc + 1) * 2 * DC],
                scalar1=gam_s[:, cc:cc + 1], scalar2=None, op0=ALU.mult),
                waits=[("sT", bkv_mm), wkvT_ld], inc=("sV", 1))
        tk["wkvT"].wrote("sV", v)

    v_onesb = ld(onesb_s[:, :], ones_i[:, :])
    v_bor = ld(borrow_s[:, :], bor_d.rearrange("(b a) -> b a", b=1))
    v = E("vector", lambda e: e.tensor_copy(borbf_s[:, :], borrow_s[:, :]),
          waits=[v_bor], inc=("sV", 1))
    tk["borbf"].wrote("sV", v)
    for it in range(NT):
        v = E("vector", lambda e, it=it: e.tensor_scalar(
            out=V_s[:, it * VST + DC: it * VST + DC + 1], in0=gam_s[:, 0:1],
            scalar1=0.0, scalar2=1.0, op0=ALU.mult, op1=ALU.add),
            waits=[v_gam0] + tk["V"].war(), inc=("sV", 1))
    tk["V"].wrote("sV", v)

    w2s1_r = w2s1.rearrange("(a b) d -> b a d", b=128)
    w2p_r = w2p.rearrange("(a b) d -> b a d", b=128)
    cast_tok = {}
    s1tok = {}
    ptok = {}
    s1c_n = [0]

    def d_cast(dst, src, c0, c1):
        def go():
            cast_tok[(dst.tensor.name, c0)] = p.dma("gpsimd", dst[:, c0:c1], src[:, c0:c1])
        return (None, go)

    def d_piece_t(wn, dcs, r, pc):
        b = pc % 2
        def go():
            ct = cast_tok[(wn.tensor.name, (dcs // 4) * 512)]
            v1 = p.dma("s2", strip_b[b][:, :],
                       wn[r * 2048:(r + 1) * 2048, dcs * 128:(dcs + 1) * 128],
                       waits=[ct] + tk[f"strip{b}"].war(), transpose=True)
            tk[f"strip{b}"].wrote(*v1)
            ptok[pc] = v1
        return (None, go)

    def d_piece_s(wt, dcs, r, pc):
        b = pc % 2
        def go():
            v2 = p.dma("vdma", wt[dcs * 128:(dcs + 1) * 128, r * 2048:(r + 1) * 2048],
                       strip_b[b][:, :], waits=[ptok[pc]])
            tk[f"strip{b}"].read(*v2)
        return (lambda: pc in ptok, go)

    def d_s1(a):
        ib = a % 2
        def go():
            ct = cast_tok[("w2n", (a // 16) * 2048)]
            vin = p.dma("s2", s1i[ib][:, :], w2n[0:D, a * 128:(a + 1) * 128],
                        waits=[ct] + tk[f"s1i{ib}"].war(), transpose=True)
            tk[f"s1i{ib}"].wrote(*vin)
            s1tok[a] = vin
        return ((lambda: s1c_n[0] >= a - 1 and ("w2n", (a // 16) * 2048) in cast_tok), go)

    def c_s1(a):
        ib = a % 2
        def go():
            vin = s1tok[a]
            for nq in range(4):
                vmm = E("tensor", lambda e, ib=ib, nq=nq: e.matmul(
                    pO[:, nq * 512:(nq + 1) * 512], h128_s[:, :],
                    s1i[ib][:, nq * 512:(nq + 1) * 512],
                    start=True, stop=True),
                    waits=[vin] + po_war(), inc=("sT", 1))
            for trk in tkpo:
                trk.wrote("sT", vmm)
            tk[f"s1i{ib}"].read("sT", vmm)
            vev = E("vector", lambda e, ib=ib: e.tensor_copy(s1o[ib][:, :], pO[:, :]),
                    waits=[("sT", vmm)] + tk[f"s1o{ib}"].war(), inc=("sV", 1))
            for trk in tkpo:
                trk.read("sV", vev)
            tk[f"s1o{ib}"].wrote("sV", vev)
            vst = p.dma("vdma", w2s1[a * 128:(a + 1) * 128, :], s1o[ib][:, :],
                        waits=[("sV", vev)])
            tk[f"s1o{ib}"].read(*vst)
            s1c_n[0] += 1
        return (lambda: a in s1tok, go)

    def c_s2k(k):
        ib = k % 2
        def go():
            gate = p.dma_all_done("vdma") if k == 0 else []
            for j in range(NBT):
                v2 = p.dma("s2", s1i[ib][j * A_BLK:(j + 1) * A_BLK, :], w2s1_r[k * NBT + j],
                           waits=(gate + tk[f"s1i{ib}"].war()) if j == 0 else [])
                tk[f"s1i{ib}"].wrote(*v2)
            for nq in range(4):
                vmm = E("tensor", lambda e, ib=ib, nq=nq: e.matmul(
                    pO[:, nq * 512:(nq + 1) * 512], hbd_s[:, :],
                    s1i[ib][:, nq * 512:(nq + 1) * 512],
                    start=True, stop=True),
                    waits=tk[f"s1i{ib}"].raw() + po_war(), inc=("sT", 1))
            for trk in tkpo:
                trk.wrote("sT", vmm)
            tk[f"s1i{ib}"].read("sT", vmm)
            vev = E("scalar", lambda e, ib=ib: e.activation(
                out=s1o[ib][:, :], in_=pO[:, :], func=AF.Copy, scale=FWHT_SCALE),
                waits=[("sT", vmm)] + tk[f"s1o{ib}"].war(), inc=("sC", 1))
            for trk in tkpo:
                trk.read("sC", vev)
            tk[f"s1o{ib}"].wrote("sC", vev)
            for j in range(NBT):
                vb = p.dma("vdma", w2p_r[k * NBT + j], s1o[ib][j * A_BLK:(j + 1) * A_BLK, :],
                           waits=[("sC", vev)] if j == 0 else [])
                tk[f"s1o{ib}"].read(*vb)
        return (lambda: s1c_n[0] == A_BLK, go)

    DQa = [d_s1(a) for a in range(A_BLK)]
    DQb = []
    DQb += [d_cast(w2n, w2_d, j * 2048, (j + 1) * 2048) for j in range(4)]
    DQb += [d_cast(w1n, w1_d, j * 512, (j + 1) * 512) for j in range(4)]
    DQb += [d_cast(w3n, w3_d, j * 512, (j + 1) * 512) for j in range(4)]
    wpieces = []
    pc = 0
    for wn, wt in ((w1n, w1t), (w3n, w3t)):
        for r in range(4):
            for dcs in range(16):
                wpieces.append((d_piece_t(wn, dcs, r, pc), d_piece_s(wt, dcs, r, pc)))
                pc += 1
    for i, (tpt, tps) in enumerate(wpieces):
        DQb.append(tpt)
        if i >= 1:
            DQb.append(wpieces[i - 1][1])
    DQb.append(wpieces[-1][1])
    CQ = [c_s1(a) for a in range(A_BLK)] + [c_s2k(k) for k in range(128 // NBT)]
    dqa_i = [0]
    dqb_i = [0]
    cq_i = [0]

    def run_q(q, qi, cap):
        n = 0
        while qi[0] < len(q) and n < cap:
            rdy, fn = q[qi[0]]
            if rdy is not None and not rdy():
                break
            fn()
            qi[0] += 1
            n += 1

    def dslot():
        run_q(DQb, dqb_i, 5)
        run_q(DQa, dqa_i, 3)

    def cslot():
        dslot()
        run_q(CQ, cq_i, 6)
        run_q(DQa, dqa_i, 3)

    prep_end = [(s, v) for s, v in p.cnt.items() if v]

    st_A2 = ExitStack()
    scores_s = st_A2.enter_context(nc.sbuf_tensor("scores_s", [128, D], F32))
    expb = [st_A2.enter_context(nc.sbuf_tensor(f"expb{i}", [128, 512], BF16)) for i in range(2)]
    for s, v in prep_end:
        tk["scores"].read(s, v)
        tk["expb0"].read(s, v)
        tk["expb1"].read(s, v)

    xmid_store = [None] * NT
    xln2_store = [None] * NT

    def pbf_transpose(src_ap, dst_ap, slot, src_ready, evac_eng="vector"):
        trk = pbf_slot[slot]
        sl = p_bf[:, slot * 128:(slot + 1) * 128]
        vt = E("tensor", lambda e, s=src_ap, sl=sl: e.transpose(sl, s, ident_s[:, :]),
               waits=src_ready + trk.war(), inc=("sT", 1))
        trk.wrote("sT", vt)
        sem = "sV" if evac_eng == "vector" else "sC"
        if evac_eng == "vector":
            vv = E(evac_eng, lambda e, d=dst_ap, sl=sl: e.tensor_copy(d, sl),
                   waits=[("sT", vt)], inc=(sem, 1))
        else:
            vv = E(evac_eng, lambda e, d=dst_ap, sl=sl: e.copy(d, sl),
                   waits=[("sT", vt)], inc=(sem, 1))
        trk.read(sem, vv)
        return sem, vv

    def layer_norm(src_ap, junk_ap, junk_trk, src_waits, out_ap, out_trk, scol):
        Dn = src_ap.shape[-1]
        st = stat_s
        trk = stt(scol)
        w_st = trk.war()
        v1 = E("vector", lambda e: e.reduce_sum(st[:, scol:scol + 1], src_ap, axis=AX.X,
                                                negate=True),
               waits=src_waits + w_st, inc=("sV", 1))
        v2 = E("scalar", lambda e: e.activation(
            out=junk_ap, in_=src_ap, func=AF.Square,
            accum_out=st[:, scol + 1:scol + 2]),
            waits=src_waits + w_st + junk_trk.war(), inc=("sC", 1))
        junk_trk.wrote("sC", v2)
        v3 = E("vector", lambda e: e.tensor_scalar(
            out=st[:, scol + 2:scol + 4], in0=st[:, scol:scol + 2], scalar1=1.0 / Dn,
            scalar2=None, op0=ALU.mult), waits=[("sV", v1), ("sC", v2)], inc=("sV", 1))
        v5 = E("vector", lambda e: e.scalar_tensor_tensor(
            out=st[:, scol + 4:scol + 5], in0=st[:, scol + 2:scol + 3],
            scalar=st[:, scol + 2:scol + 3], in1=st[:, scol + 3:scol + 4],
            op0=ALU.mult, op1=ALU.subtract),
            waits=[("sV", v3)], inc=("sV", 1))
        v6 = E("vector", lambda e: e.tensor_scalar(
            out=st[:, scol + 4:scol + 5], in0=st[:, scol + 4:scol + 5], scalar1=-1.0,
            scalar2=LN_EPS, op0=ALU.mult, op1=ALU.add), waits=[("sV", v5)], inc=("sV", 1))
        v7 = E("scalar", lambda e: e.activation(
            out=st[:, scol + 5:scol + 6], in_=st[:, scol + 4:scol + 5], func=AF.Ln),
            waits=[("sV", v6)], inc=("sC", 1))
        v8 = E("scalar", lambda e: e.activation(
            out=st[:, scol + 5:scol + 6], in_=st[:, scol + 5:scol + 6], func=AF.Exp,
            scale=-0.5), waits=[("sC", v7)], inc=("sC", 1))
        v10 = E("vector", lambda e: e.tensor_scalar(
            out=out_ap, in0=src_ap, scalar1=st[:, scol + 2:scol + 3],
            scalar2=st[:, scol + 5:scol + 6], op0=ALU.add, op1=ALU.mult),
            waits=[("sC", v8)] + src_waits + out_trk.war(), inc=("sV", 1))
        out_trk.wrote("sV", v10)
        trk.wrote("sC", v8)
        trk.read("sV", v10)
        trk.read("sV", v3)
        return v10

    if phase_a:
        def a1_tile(g, t):
            it = g * 4 + t
            xb, xbn = xa_b[t % 2], f"xa{t % 2}"
            vx = p.dma("sync", xb[:, :], x_d[it * 128:(it + 1) * 128, :],
                       waits=tk[xbn].war())
            tk[xbn].wrote(*vx)
            vln = layer_norm(xb[:, :], xmid_s[:, :], tk["xmid"],
                             [vx], xlnbf_s[:, :], tk["xlnbf"], (t % 2) * 8)
            tk[xbn].read("sV", vln)
            dslot()
            for dc in range(16):
                sem, vv = pbf_transpose(
                    xlnbf_s[:, dc * 128:(dc + 1) * 128],
                    xlnT_s[:, dc * 512 + t * 128: dc * 512 + t * 128 + 128],
                    (t * 16 + dc) % 8, tk["xlnbf"].raw() + tk["xlnT"].war(),
                    evac_eng=("vector" if dc % 2 == 0 else "scalar"))
                tk["xlnbf"].read("sT", p.cnt["sT"])
                tk["xlnT"].wrote(sem, vv)

        def a1_kvlat(g):
            for lc in range(4):
                for dc in range(16):
                    vmm = E("tensor", lambda e, lc=lc, dc=dc: e.matmul(
                        pX[:, :], wtkvT_s[:, dc * DL + lc * 128: dc * DL + lc * 128 + 128],
                        xlnT_s[:, dc * 512:(dc + 1) * 512],
                        start=(dc == 0), stop=(dc == 15)),
                        waits=(tk["xlnT"].raw() + tk["pX"].war() +
                               ([wtkvT_done] if g == 0 else [])),
                        inc=("sT", 1))
                tk["pX"].wrote("sT", vmm)
                vev = E("scalar", lambda e, lc=lc: e.copy(
                    kvlT_s[:, lc * 512:(lc + 1) * 512], pX[:, :]),
                    waits=[("sT", vmm)] + tk["kvlT"].war(), inc=("sC", 1))
                tk["pX"].read("sC", vev)
                tk["kvlT"].wrote("sC", vev)

        def a1_c(g, t):
            for lc in range(4):
                vmm = E("tensor", lambda e, t=t, lc=lc: e.matmul(
                    pX[:, 0:DC], kvlT_s[:, lc * 512 + t * 128: lc * 512 + t * 128 + 128],
                    wcT_s[:, lc * DC:(lc + 1) * DC],
                    start=(lc == 0), stop=(lc == 3)),
                    waits=(tk["kvlT"].raw() + tk["pX"].war() +
                           ([wcT_done] if g == 0 else [])),
                    inc=("sT", 1))
            tk["pX"].wrote("sT", vmm)
            vev = E("scalar", lambda e: e.copy(c_f32[:, :], pX[:, 0:DC]),
                    waits=[("sT", vmm)] + tk["c_f32"].war(), inc=("sC", 1))
            tk["pX"].read("sC", vev)
            tk["c_f32"].wrote("sC", vev)
            vln = layer_norm(c_f32[:, :], qb_s[:, 0:DC], tk["qb"],
                             [("sC", vev)], cnbf_s[:, :], tk["cnbf"], 16 + (t % 2) * 8)
            tk["c_f32"].read("sV", vln)
            for cc in range(2):
                sem, vv = pbf_transpose(
                    cnbf_s[:, cc * 128:(cc + 1) * 128],
                    cnT_s[:, cc * 512 + t * 128: cc * 512 + t * 128 + 128],
                    (t * 2 + cc) % 8, tk["cnbf"].raw() + tk["cnT"].war(),
                    evac_eng="vector")
                tk["cnbf"].read("sT", p.cnt["sT"])
                tk["cnT"].wrote(sem, vv)

        def a1_pieces(g):
            ps = [lambda t=t: a1_tile(g, t) for t in range(4)]
            ps.append(lambda: a1_kvlat(g))
            ps += [lambda t=t: a1_c(g, t) for t in range(4)]
            return ps

        def a1_qkv(g):
            vcs = p.dma("sync", cosg_s[:, :], cosT_d[:, g * 512:(g + 1) * 512],
                        waits=tk["cosg"].war())
            tk["cosg"].wrote(*vcs)
            vsn = p.dma("sync", sing_s[:, :], sinT_d[:, g * 512:(g + 1) * 512],
                        waits=tk["sing"].war())
            tk["sing"].wrote(*vsn)
            for qc in range(2):
                pb = (pS0, "pS0") if qc % 2 == 0 else (pS1, "pS1")
                for cc in range(2):
                    vmm = E("tensor", lambda e, qc=qc, cc=cc, pb=pb: e.matmul(
                        pb[0][:, :], wqT_s[:, cc * DC + qc * 128: cc * DC + qc * 128 + 128],
                        cnT_s[:, cc * 512:(cc + 1) * 512],
                        start=(cc == 0), stop=(cc == 1)),
                        waits=tk["cnT"].raw() + tk[pb[1]].war() + (tk["wqT"].raw() if g == 0 else []),
                        inc=("sT", 1))
                tk[pb[1]].wrote("sT", vmm)
                vev = E("vector", lambda e, qc=qc, pb=pb: e.tensor_scalar(
                    out=qb_s[:, qc * 512:(qc + 1) * 512], in0=pb[0][:, :],
                    scalar1=bq_s[:, qc:qc + 1], scalar2=None, op0=ALU.add),
                    waits=[("sT", vmm), ("sC", BE[0])] + tk["qb"].war(), inc=("sV", 1))
                tk[pb[1]].read("sV", vev)
                tk["qb"].wrote("sV", vev)
            cosg = cosg_s[:, :]
            sing = sing_s[:, :]
            v1 = E("vector", lambda e: e.tensor_tensor(
                out=ropet0[:, :], in0=qb_s[:, 0:512], in1=cosg, op=ALU.mult),
                waits=tk["qb"].raw() + tk["rt0"].war() + [vcs, vsn], inc=("sV", 1))
            tk["rt0"].wrote("sV", v1)
            v2 = E("vector", lambda e: e.tensor_tensor(
                out=ropet1[:, :], in0=qb_s[:, 512:1024], in1=sing, op=ALU.mult),
                waits=tk["qb"].raw() + tk["rt1"].war(), inc=("sV", 1))
            tk["rt1"].wrote("sV", v2)
            v3 = E("vector", lambda e: e.tensor_tensor(
                out=qT_s[:, 0:512], in0=ropet0[:, :], in1=ropet1[:, :], op=ALU.subtract),
                waits=[("sV", v2)] + tk["qT"].war(), inc=("sV", 1))
            v4 = E("vector", lambda e: e.tensor_tensor(
                out=ropet0[:, :], in0=qb_s[:, 512:1024], in1=cosg, op=ALU.mult),
                waits=[("sV", v3)], inc=("sV", 1))
            v5 = E("vector", lambda e: e.tensor_tensor(
                out=ropet1[:, :], in0=qb_s[:, 0:512], in1=sing, op=ALU.mult),
                waits=[("sV", v4)], inc=("sV", 1))
            v6 = E("vector", lambda e: e.tensor_tensor(
                out=qT_s[:, 512:1024], in0=ropet0[:, :], in1=ropet1[:, :], op=ALU.add),
                waits=[("sV", v5)], inc=("sV", 1))
            tk["qT"].wrote("sV", v6)
            tk["qb"].read("sV", v5)
            tk["rt0"].wrote("sV", v4)
            tk["rt1"].wrote("sV", v5)
            for kc in range(2):
                pb = (pS0, "pS0") if kc % 2 == 0 else (pS1, "pS1")
                for cc in range(2):
                    vmm = E("tensor", lambda e, kc=kc, cc=cc, pb=pb: e.matmul(
                        pb[0][:, :], wkvT_s[:, cc * 2 * DC + kc * 128: cc * 2 * DC + kc * 128 + 128],
                        cnT_s[:, cc * 512:(cc + 1) * 512],
                        start=(cc == 0), stop=(cc == 1)),
                        waits=tk["cnT"].raw() + tk[pb[1]].war() + (tk["wkvT"].raw() if g == 0 else []),
                        inc=("sT", 1))
                tk[pb[1]].wrote("sT", vmm)
                vev = E("vector", lambda e, kc=kc, pb=pb: e.tensor_scalar(
                    out=qb_s[:, kc * 512:(kc + 1) * 512], in0=pb[0][:, :],
                    scalar1=bkv_s[:, kc:kc + 1], scalar2=None, op0=ALU.add),
                    waits=[("sT", vmm), ("sC", BE[0])] + tk["qb"].war(), inc=("sV", 1))
                tk[pb[1]].read("sV", vev)
                tk["qb"].wrote("sV", vev)
            kdst0 = KT_s[:, 0 * T + g * 512: 0 * T + g * 512 + 512]
            kdst1 = KT_s[:, 1 * T + g * 512: 1 * T + g * 512 + 512]
            v1 = E("vector", lambda e: e.tensor_tensor(
                out=ropet0[:, :], in0=qb_s[:, 0:512], in1=cosg, op=ALU.mult),
                waits=tk["qb"].raw() + tk["rt0"].war(), inc=("sV", 1))
            v2 = E("vector", lambda e: e.tensor_tensor(
                out=ropet1[:, :], in0=qb_s[:, 512:1024], in1=sing, op=ALU.mult),
                waits=tk["qb"].raw() + tk["rt1"].war(), inc=("sV", 1))
            v3 = E("vector", lambda e, kdst0=kdst0: e.tensor_tensor(
                out=kdst0, in0=ropet0[:, :], in1=ropet1[:, :], op=ALU.subtract),
                waits=[("sV", v2)] + tk["KT"].war(), inc=("sV", 1))
            v4 = E("vector", lambda e: e.tensor_tensor(
                out=ropet0[:, :], in0=qb_s[:, 512:1024], in1=cosg, op=ALU.mult),
                waits=[("sV", v3)], inc=("sV", 1))
            v5 = E("vector", lambda e: e.tensor_tensor(
                out=ropet1[:, :], in0=qb_s[:, 0:512], in1=sing, op=ALU.mult),
                waits=[("sV", v4)], inc=("sV", 1))
            v6 = E("vector", lambda e, kdst1=kdst1: e.tensor_tensor(
                out=kdst1, in0=ropet0[:, :], in1=ropet1[:, :], op=ALU.add),
                waits=[("sV", v5)], inc=("sV", 1))
            tk["KT"].wrote("sV", v6)
            tk["qb"].read("sV", v5)
            tk["rt0"].wrote("sV", v4)
            tk["rt1"].wrote("sV", v5)
            tk["cosg"].read("sV", v6)
            tk["sing"].read("sV", v6)
            for kc in (2, 3):
                pb = (pS0, "pS0") if kc % 2 == 0 else (pS1, "pS1")
                for cc in range(2):
                    vmm = E("tensor", lambda e, kc=kc, cc=cc, pb=pb: e.matmul(
                        pb[0][:, :], wkvT_s[:, cc * 2 * DC + kc * 128: cc * 2 * DC + kc * 128 + 128],
                        cnT_s[:, cc * 512:(cc + 1) * 512],
                        start=(cc == 0), stop=(cc == 1)),
                        waits=tk["cnT"].raw() + tk[pb[1]].war() + (tk["wkvT"].raw() if g == 0 else []),
                        inc=("sT", 1))
                tk[pb[1]].wrote("sT", vmm)
                vev = E("vector", lambda e, kc=kc, pb=pb: e.tensor_scalar(
                    out=vbf_s[:, :], in0=pb[0][:, :],
                    scalar1=bkv_s[:, kc:kc + 1], scalar2=None, op0=ALU.add),
                    waits=[("sT", vmm), ("sC", BE[0])] + tk["vbf"].war(), inc=("sV", 1))
                tk[pb[1]].read("sV", vev)
                tk["vbf"].wrote("sV", vev)
                dvc = kc - 2
                for t in range(4):
                    it = g * 4 + t
                    sem, vv = pbf_transpose(
                        vbf_s[:, t * 128:(t + 1) * 128],
                        V_s[:, it * VST + dvc * 128: it * VST + dvc * 128 + 128],
                        (4 + t) % 8, tk["vbf"].raw() + tk["V"].war(), evac_eng="vector")
                    tk["vbf"].read("sT", p.cnt["sT"])
                    tk["V"].wrote(sem, vv)

        def emit_scores_exp(g, kb):
            b = kb % 2
            ps, psn = (pS0, "pS0") if kb % 2 == 0 else (pS1, "pS1")
            for cc in range(2):
                vmm = E("tensor", lambda e, kb=kb, cc=cc, ps=ps: e.matmul(
                    ps[:, :], KT_s[:, cc * T + kb * 128: cc * T + kb * 128 + 128],
                    qT_s[:, cc * 512:(cc + 1) * 512],
                    start=(cc == 0), stop=(cc == 1)),
                    waits=tk["qT"].raw() + tk["KT"].raw() + tk[psn].war(),
                    inc=("sT", 1))
            tk[psn].wrote("sT", vmm)
            cut = max(0, kb - g * 4) * 128
            if kb >= g * 4:
                t0 = kb - g * 4
                vm = E("vector", lambda e, t0=t0, ps=ps: e.tensor_tensor(
                    out=scores_s[:, 0:128], in0=ps[:, t0 * 128:(t0 + 1) * 128],
                    in1=mask_s[:, :], op=ALU.add),
                    waits=[("sT", vmm)] + tk["scores"].war(), inc=("sV", 1))
                tk[psn].read("sV", vm)
                tk["scores"].wrote("sV", vm)
                vex = E("scalar", lambda e, b=b, t0=t0: e.activation(
                    out=expb[b][:, t0 * 128:(t0 + 1) * 128], in_=scores_s[:, 0:128],
                    func=AF.Exp, scale=SCALE),
                    waits=[("sV", vm)] + tk[f"expb{b}"].war(), inc=("sC", 1))
                tk["scores"].read("sC", vex)
                tk[f"expb{b}"].wrote("sC", vex)
                if t0 < 3:
                    vex = E("scalar", lambda e, b=b, t0=t0, ps=ps: e.activation(
                        out=expb[b][:, (t0 + 1) * 128:512], in_=ps[:, (t0 + 1) * 128:512],
                        func=AF.Exp, scale=SCALE),
                        waits=[("sT", vmm)], inc=("sC", 1))
                    tk[psn].read("sC", vex)
                    tk[f"expb{b}"].wrote("sC", vex)
            else:
                vex = E("scalar", lambda e, b=b, ps=ps: e.activation(
                    out=expb[b][:, :], in_=ps[:, :], func=AF.Exp, scale=SCALE),
                    waits=[("sT", vmm)] + tk[f"expb{b}"].war(), inc=("sC", 1))
                tk[psn].read("sC", vex)
                tk[f"expb{b}"].wrote("sC", vex)
            return kb, vex, cut

        def emit_avs(g, st):
            kb, vex, cut = st
            b = kb % 2
            for t in range(cut // 128, 4):
                it = g * 4 + t
                vav = E("tensor", lambda e, b=b, t=t, kb=kb: e.matmul(
                    pO[:, t * 512: t * 512 + DC + 1], expb[b][:, t * 128:(t + 1) * 128],
                    V_s[:, kb * VST: kb * VST + DC + 1],
                    start=(kb == 0), stop=(kb == it), skip_group_check=True),
                    waits=[("sC", vex)] + tk["V"].raw() +
                          (tkpo[t].war() if kb == 0 else []),
                    inc=("sT", 1))
                tk[f"expb{b}"].read("sT", vav)
                if kb == it:
                    tkpo[t].wrote("sT", vav)
                    vrc = E("vector", lambda e, t=t: e.reciprocal(
                        stat_s[:, 48 + t:49 + t], pO[:, t * 512 + DC: t * 512 + DC + 1]),
                        waits=[("sT", vav)] + stt(48 + t).war(), inc=("sV", 1))
                    tkpo[t].read("sV", vrc)
                    stt(48 + t).wrote("sV", vrc)
                    vo = E("scalar", lambda e, t=t: e.activation(
                        out=o_s[:, t * DC:(t + 1) * DC], in_=pO[:, t * 512: t * 512 + DC],
                        func=AF.Copy, scale=stat_s[:, 48 + t:49 + t]),
                        waits=[("sV", vrc)] + tk["o"].war(), inc=("sC", 1))
                    tkpo[t].read("sC", vo)
                    tk["o"].wrote("sC", vo)
                    stt(48 + t).read("sC", vo)

        def outproj_tile(g, t):
            it = g * 4 + t
            xb, xbn = xa_b[t % 2], f"xa{t % 2}"
            for nq in range(4):
                for lc in range(4):
                    vmm = E("tensor", lambda e, t=t, nq=nq, lc=lc: e.matmul(
                        pO[:, nq * 512:(nq + 1) * 512],
                        zT_s[:, lc * 512 + t * 128: lc * 512 + t * 128 + 128],
                        wcomb_s[:, lc * D + nq * 512: lc * D + nq * 512 + 512],
                        start=(lc == 0), stop=False),
                        waits=(tk["zT"].raw() +
                               (tkpo[nq].war() if lc == 0 else []) +
                               ([("sV", wcomb_done)] if g == 0 else [])),
                        inc=("sT", 1))
                vmm = E("tensor", lambda e, nq=nq: e.matmul(
                    pO[:, nq * 512:(nq + 1) * 512], onesb_s[0:1, 0:128],
                    borbf_s[0:1, nq * 512:(nq + 1) * 512],
                    start=False, stop=True),
                    waits=([v_onesb] + tk["borbf"].raw()) if g == 0 else [],
                    inc=("sT", 1))
                tkpo[nq].wrote("sT", vmm)
            vx = p.dma("sync", xb[:, :], x_d[it * 128:(it + 1) * 128, :],
                       waits=tk[xbn].war())
            tk[xbn].wrote(*vx)
            vadd = E("vector", lambda e, xb=xb: e.tensor_tensor(
                out=xmid_s[:, :], in0=pO[:, :], in1=xb[:, :], op=ALU.add),
                waits=[("sT", vmm), vx] + tk["xmid"].war(), inc=("sV", 1))
            for trk in tkpo:
                trk.read("sV", vadd)
            tk[xbn].read("sV", vadd)
            tk["xmid"].wrote("sV", vadd)
            vln = layer_norm(xmid_s[:, :], scores_s[:, 0:D], tk["scores"],
                             [("sV", vadd)], xlnbf_s[:, :], tk["xlnbf"], 32 + (t % 2) * 8)
            tk["xmid"].read("sV", vln)
            vst1 = p.dma("gpsimd", xmid_d[it * 128:(it + 1) * 128, :], xmid_s[:, :],
                         waits=[("sV", vln)])
            tk["xmid"].read(*vst1)
            xmid_store[it] = vst1
            vst2 = p.dma("gpsimd", xln2_d[it * 128:(it + 1) * 128, :], xlnbf_s[:, :],
                         waits=[("sV", vln)])
            tk["xlnbf"].read(*vst2)
            xln2_store[it] = vst2
            cslot()

        for f in a1_pieces(0):
            f()
        bias_prep()
        a1_qkv(0)
        for g in range(NG):
            nxt = a1_pieces(g + 1) if g + 1 < NG else []
            ni = [0]

            def next_a1(nxt=nxt, ni=ni):
                if ni[0] < len(nxt):
                    nxt[ni[0]]()
                    ni[0] += 1

            nkb = g * 4 + 4
            npc = len(nxt)
            av_pending = None
            for kb in range(nkb):
                st_kb = emit_scores_exp(g, kb)
                if av_pending is not None:
                    emit_avs(g, av_pending)
                av_pending = st_kb
                while ni[0] < npc and ni[0] * nkb <= kb * npc:
                    next_a1()
            emit_avs(g, av_pending)
            for t in range(4):
                for dvc in range(2):
                    sem, vv = pbf_transpose(
                        o_s[:, t * DC + dvc * 128: t * DC + dvc * 128 + 128],
                        oT_s[:, dvc * 512 + t * 128: dvc * 512 + t * 128 + 128],
                        (t * 2 + dvc) % 8, tk["o"].raw() + tk["oT"].war(), evac_eng="vector")
                    tk["o"].read("sT", p.cnt["sT"])
                    tk["oT"].wrote(sem, vv)
            next_a1()
            for lc in range(4):
                pb = (pS0, "pS0") if lc % 2 == 0 else (pS1, "pS1")
                for dvc in range(2):
                    vmm = E("tensor", lambda e, lc=lc, dvc=dvc, pb=pb: e.matmul(
                        pb[0][:, :], wc2lT_s[:, dvc * DL + lc * 128: dvc * DL + lc * 128 + 128],
                        oT_s[:, dvc * 512:(dvc + 1) * 512],
                        start=(dvc == 0), stop=(dvc == 1)),
                        waits=(tk["oT"].raw() + tk[pb[1]].war() +
                               ([wc2lT_done] if g == 0 else [])),
                        inc=("sT", 1))
                tk[pb[1]].wrote("sT", vmm)
                vev = E("scalar", lambda e, lc=lc, pb=pb: e.copy(
                    zT_s[:, lc * 512:(lc + 1) * 512], pb[0][:, :]),
                    waits=[("sT", vmm)] + tk["zT"].war(), inc=("sC", 1))
                tk[pb[1]].read("sC", vev)
                tk["zT"].wrote("sC", vev)
            next_a1()
            for t in range(4):
                outproj_tile(g, t)
                next_a1()
            while ni[0] < npc:
                next_a1()
            if g + 1 < NG:
                a1_qkv(g + 1)
        run_q(DQa, dqa_i, 10 ** 9)
        run_q(DQb, dqb_i, 10 ** 9)
        run_q(CQ, cq_i, 10 ** 9)
    w13t_all = p.dma_all_done("vdma")
    w2p_all = p.dma_all_done("vdma")

    st_A2.close()
    st_psA.close()
    st_A.close()

    A_end = [(s, v) for s, v in p.cnt.items() if v]

    if phase_b:
        st_B = ExitStack()
        xln2T_s = st_B.enter_context(nc.sbuf_tensor("xln2T_s", [128, 16 * C], BF16))
        oacc = [st_B.enter_context(nc.sbuf_tensor(f"oacc{i}", [128, D], F32)) for i in range(C // 128)]
        hid_s = st_B.enter_context(nc.sbuf_tensor("hid_s", [128, (HG // 128) * C], BF16))
        w1sl = [st_B.enter_context(nc.sbuf_tensor(f"w1sl{i}", [128, 16 * 256], BF16)) for i in range(2)]
        w3sl = [st_B.enter_context(nc.sbuf_tensor(f"w3sl{i}", [128, 16 * 256], BF16)) for i in range(2)]
        w2g_s = st_B.enter_context(nc.sbuf_tensor("w2g_s", [128, (HG // 128) * D], BF16))
        sg_s = st_B.enter_context(nc.sbuf_tensor("sg_s", [128, C], F32))
        xbm = [st_B.enter_context(nc.sbuf_tensor(f"xbm{i}", [128, D], F32)) for i in range(2)]

        pG = st_B.enter_context(nc.psum_tensor("pG", [128, C], F32))
        pL = st_B.enter_context(nc.psum_tensor("pL", [128, C], F32))
        pDa = st_B.enter_context(nc.psum_tensor("pDa", [128, 1024], F32))
        pDb = st_B.enter_context(nc.psum_tensor("pDb", [128, 1024], F32))
        for nm in ("xln2T", "w2g", "sg", "xbm", "xbm2", "pG", "pL", "pDa", "pDb"):
            for s, v in A_end:
                tk[nm].read(s, v)

        w1t_r = w1t.rearrange("(dc p) h -> p dc h", p=128)
        w3t_r = w3t.rearrange("(dc p) h -> p dc h", p=128)
        w2p_r2 = w2p.rearrange("(hc p) d -> p hc d", p=128)

        tkoa = [Trk() for _ in range(C // 128)]
        tkhid = [Trk() for _ in range(HG // 128)]
        w1trk = [Trk() for _ in range(2)]
        w3trk = [Trk() for _ in range(2)]
        for trk in tkoa + tkhid + w1trk + w3trk:
            for s, v in A_end:
                trk.read(s, v)

        def load_xln2T(sw):
            c0 = sw * C
            gate = [xln2_store[c0 // 128 + k] for k in range(C // 128)] if phase_a else []
            for dc in range(16):
                vv = p.dma("sync", xln2T_s[:, dc * C:(dc + 1) * C],
                           xln2_d[c0:c0 + C, dc * 128:(dc + 1) * 128],
                           waits=gate + tk["xln2T"].war(), transpose=True)
                tk["xln2T"].wrote(*vv)
            return vv

        xln2T_pending = load_xln2T(0)
        for sw in range(NSW):
            c0 = sw * C
            xmid_tok = [None] * (C // 128)
            xln2T_ready = xln2T_pending
            if debug and sw == 0:
                for dc in range(16):
                    vdbg = p.dma("gpsimd", dbgXT[dc * 128:(dc + 1) * 128, 0:C],
                                 xln2T_s[:, dc * C:(dc + 1) * C], waits=[xln2T_ready])
                    tk["xln2T"].read(*vdbg)

            NHG = HID // HG
            for hg in range(NHG):
                h0 = hg * HG
                NSL = HG // 256
                slice_ready = [None] * NSL

                def load_slice(sl4, first=False):
                    b = sl4 % 2
                    hh = h0 + sl4 * 256
                    va = p.dma("sync", w1sl[b][:, :], w1t_r[:, :, hh:hh + 256],
                               waits=(w13t_all if first else []) + w1trk[b].war())
                    w1trk[b].wrote(*va)
                    vb = p.dma("sync", w3sl[b][:, :], w3t_r[:, :, hh:hh + 256],
                               waits=w3trk[b].war())
                    w3trk[b].wrote(*vb)
                    slice_ready[sl4] = (va, vb)
                    if debug and sw == 0 and hg == 0 and sl4 == 0:
                        vdbg = p.dma("gpsimd", dbgW1[:, :], w1sl[b][:, :], waits=[va])
                        w1trk[b].read(*vdbg)

                load_slice(0, first=(sw == 0 and hg == 0))
                if NSL > 1:
                    load_slice(1)
                vw2 = p.dma("sync", w2g_s[:, :], w2p_r2[:, h0 // 128:(h0 + HG) // 128, :],
                            waits=(w2p_all if (sw == 0 and hg == 0) else []) + tk["w2g"].war())
                tk["w2g"].wrote(*vw2)
                for hc in range(HG // 128):
                    b = (hc // 2) % 2
                    va, vb = slice_ready[hc // 2]
                    col = (hc % 2) * 128
                    for half in range(C // 512):
                        for dc in range(16):
                            vg = E("tensor", lambda e, b=b, col=col, dc=dc, half=half: e.matmul(
                                pG[:, half * 512:(half + 1) * 512],
                                w1sl[b][:, dc * 256 + col: dc * 256 + col + 128],
                                xln2T_s[:, dc * C + half * 512: dc * C + half * 512 + 512],
                                start=(dc == 0), stop=(dc == 15)),
                                waits=([va, xln2T_ready] + tk["pG"].war()) if (dc == 0 and half == 0) else [],
                                inc=("sT", 1))
                    tk["pG"].wrote("sT", vg)
                    w1trk[b].read("sT", vg)
                    for half in range(C // 512):
                        for dc in range(16):
                            vl = E("tensor", lambda e, b=b, col=col, dc=dc, half=half: e.matmul(
                                pL[:, half * 512:(half + 1) * 512],
                                w3sl[b][:, dc * 256 + col: dc * 256 + col + 128],
                                xln2T_s[:, dc * C + half * 512: dc * C + half * 512 + 512],
                                start=(dc == 0), stop=(dc == 15)),
                                waits=([vb] + tk["pL"].war()) if (dc == 0 and half == 0) else [],
                                inc=("sT", 1))
                    tk["pL"].wrote("sT", vl)
                    w3trk[b].read("sT", vl)
                    tk["xln2T"].read("sT", vl)
                    if hc % 2 == 1 and hc // 2 + 2 < NSL:
                        load_slice(hc // 2 + 2)
                    vsi = E("scalar", lambda e: e.activation(
                        out=sg_s[:, :], in_=pG[:, :], func=AF.Silu),
                        waits=[("sT", vg)] + tk["sg"].war(), inc=("sC", 1))
                    tk["pG"].read("sC", vsi)
                    tk["sg"].wrote("sC", vsi)
                    hslot = hc
                    vmu = E("vector", lambda e, hslot=hslot: e.tensor_tensor(
                        out=hid_s[:, hslot * C:(hslot + 1) * C], in0=sg_s[:, :],
                        in1=pL[:, :], op=ALU.mult),
                        waits=[("sC", vsi), ("sT", vl)] + tkhid[hslot].war(), inc=("sV", 1))
                    tk["pL"].read("sV", vmu)
                    tk["sg"].read("sV", vmu)
                    tkhid[hslot].wrote("sV", vmu)
                    if debug and sw == 0 and hg == 0:
                        vdbg = p.dma("gpsimd", dbgH[hslot * 128:(hslot + 1) * 128, 0:C],
                                     hid_s[:, hslot * C:(hslot + 1) * C],
                                     waits=[("sV", vmu)])
                        tkhid[hslot].read(*vdbg)
                if hg == NHG - 1 and sw + 1 < NSW:
                    xln2T_pending = load_xln2T(sw + 1)

                def load_xmid(tt):
                    it = (c0 // 128) + tt
                    bn = "xbm" if tt % 2 == 0 else "xbm2"
                    vxm = p.dma("sync", xbm[tt % 2][:, :], xmid_d[it * 128:(it + 1) * 128, :],
                                waits=([xmid_store[it]] if phase_a else []) + tk[bn].war())
                    tk[bn].wrote(*vxm)
                    xmid_tok[tt] = vxm

                if hg == 0:
                    load_xmid(0)
                    load_xmid(1)
                hid_ready = [tkhid[i].raw() for i in range(HG // 128)]
                for tt in range(C // 128):
                    if hg == 0 and 1 <= tt and tt + 1 < C // 128:
                        load_xmid(tt + 1)
                    for half in range(2):
                        pD = (pDa, "pDa") if (tt * 2 + half) % 2 == 0 else (pDb, "pDb")
                        for hc in range(HG // 128):
                            for n2 in range(2):
                                nd0 = half * 1024 + n2 * 512
                                vdm = E("tensor", lambda e, tt=tt, hc=hc, n2=n2, nd0=nd0, pD=pD: e.matmul(
                                    pD[0][:, n2 * 512:(n2 + 1) * 512],
                                    hid_s[:, hc * C + tt * 128: hc * C + tt * 128 + 128],
                                    w2g_s[:, hc * D + nd0: hc * D + nd0 + 512],
                                    start=(hc == 0), stop=(hc == HG // 128 - 1)),
                                    waits=(hid_ready[hc] + [vw2] + tk[pD[1]].war()) if (hc == 0 and n2 == 0) else (hid_ready[hc] if n2 == 0 else []),
                                    inc=("sT", 1))
                        tk[pD[1]].wrote("sT", vdm)
                        for hc in range(HG // 128):
                            tkhid[hc].read("sT", vdm)
                        tk["w2g"].read("sT", vdm)
                        if hg == 0:
                            bn = "xbm" if tt % 2 == 0 else "xbm2"
                            vea = E("vector", lambda e, tt=tt, half=half, pD=pD: e.tensor_tensor(
                                out=oacc[tt][:, half * 1024:(half + 1) * 1024],
                                in0=pD[0][:, :], in1=xbm[tt % 2][:, half * 1024:(half + 1) * 1024],
                                op=ALU.add),
                                waits=[("sT", vdm), xmid_tok[tt]] + tkoa[tt].war(), inc=("sV", 1))
                            tk[bn].read("sV", vea)
                        else:
                            vea = E("vector", lambda e, tt=tt, half=half, pD=pD: e.tensor_tensor(
                                out=oacc[tt][:, half * 1024:(half + 1) * 1024],
                                in0=pD[0][:, :], in1=oacc[tt][:, half * 1024:(half + 1) * 1024],
                                op=ALU.add),
                                waits=[("sT", vdm)] + tkoa[tt].war(), inc=("sV", 1))
                        tk[pD[1]].read("sV", vea)
                        tkoa[tt].wrote("sV", vea)
                        if hg == NHG - 1 and half == 1:
                            it = (c0 // 128) + tt
                            vso = p.dma("gpsimd", out_d[it * 128:(it + 1) * 128, :],
                                        oacc[tt][:, :], waits=[("sV", vea)])
                            tkoa[tt].read(*vso)
                        if debug and sw == 0 and hg == 0:
                            vdbg = p.dma("gpsimd",
                                         dbgO[tt * 128:(tt + 1) * 128, half * 1024:(half + 1) * 1024],
                                         oacc[tt][:, half * 1024:(half + 1) * 1024],
                                         waits=[("sV", vea)])
                            tkoa[tt].read(*vdbg)
        st_B.close()

    E("gpsimd", None, waits=p.dma_all_done("gpsimd") + p.dma_all_done("vdma"))
    E("sync", None, waits=p.dma_all_done("sync") + p.dma_all_done("s2"))

    with nc.Block() as block:
        block.sync(lambda e: [f(e) for f in p.ops["sync"]])
        block.scalar(lambda e: [f(e) for f in p.ops["scalar"]])
        block.vector(lambda e: [f(e) for f in p.ops["vector"]])
        block.tensor(lambda e: [f(e) for f in p.ops["tensor"]])
        block.gpsimd(lambda e: [f(e) for f in p.ops["gpsimd"]])

    if phase_b:
        st_B.close()
    return nc


def kernel(**inputs):
    def f32(a):
        return np.ascontiguousarray(np.asarray(a, dtype=np.float32))

    x = f32(inputs["L0_attn_x"])
    cosT, sinT = host_consts()
    common = {
        "wtkv": f32(inputs["L0_attn_W_to_kv"]),
        "wc": f32(inputs["L0_attn_W_compress"]),
        "gam": f32(inputs["L0_attn_ln_gamma"]),
        "bet": f32(inputs["L0_attn_ln_beta"]),
        "wq": f32(inputs["L0_attn_W_q_comp"]),
        "wkv": f32(inputs["L0_attn_W_kv_comp"]),
        "wc2l": f32(inputs["L0_attn_W_comp2lat"]),
        "wor": f32(inputs["L0_attn_W_out_res"]),
        "bor": f32(inputs["L0_attn_b_out_res"]),
        "w1": f32(inputs["L0_ffn_W1"]),
        "w2": f32(inputs["L0_ffn_W2"]),
        "w3": f32(inputs["L0_ffn_W3"]),
        "cosT": cosT,
        "sinT": sinT,
    }
    set_dims()
    nc = build()
    in_maps = [dict(common, x=x[c]) for c in range(8)]
    res = run_bass_kernel_spmd(nc, in_maps, core_ids=list(range(8)))
    return np.stack([r["out"] for r in res.results], axis=0)


if __name__ == "__main__":
    nc = build()
    print("instructions:", nc.n_instructions())


# version 24
# speedup vs baseline: 1.1675x; 1.1675x over previous
import numpy as np
import ml_dtypes
import concourse.bass as bass
import concourse.mybir as mybir
from concourse.bass_utils import run_bass_kernel_spmd
from contextlib import ExitStack

F32 = mybir.dt.float32
BF16 = mybir.dt.bfloat16
AF = mybir.ActivationFunctionType
ALU = mybir.AluOpType
AX = mybir.AxisListType

D = 2048
DL = 512
DC = 256
LN_EPS = 1e-5
SCALE = 0.0625


def set_dims(t=4096, c=1024, hid=8192, hg=1024):
    global T, NT, NG, C, NSW, HID, HG, A_BLK, NBT, FWHT_SCALE
    T = t
    NT = T // 128
    NG = NT // 4
    C = c
    NSW = T // C
    HID = hid
    HG = hg
    A_BLK = HID // 128
    NBT = 128 // A_BLK
    FWHT_SCALE = float(np.float32(1.0 / np.sqrt(np.float32(HID))))


set_dims()

ENGINES = ("sync", "scalar", "vector", "tensor", "gpsimd")


def _sylvester(n):
    h = np.array([[1.0]], dtype=np.float64)
    while h.shape[0] < n:
        h = np.block([[h, h], [h, -h]])
    return h


def host_consts():
    inv = (1.0 / (10000.0 ** (np.arange(0, DC, 2).astype(np.float32) / np.float32(DC)))).astype(np.float32)
    t = np.arange(T, dtype=np.float32)
    freqs = inv[:, None] * t[None, :]
    cosT = np.cos(freqs).astype(np.float32)
    sinT = np.sin(freqs).astype(np.float32)
    return cosT, sinT


class PB:

    NDMA = 6

    def __init__(self, nc):
        self.nc = nc
        self.ops = {e: [] for e in ENGINES}
        self.cnt = {}
        self.sems = {}
        self.wmax = {e: {} for e in ENGINES}
        self.rr = {}

    def sem(self, name):
        self.sems[name] = self.nc.alloc_semaphore(name)
        self.cnt[name] = 0

    def dma_pool(self, queue, engine=None, n=None):
        names = [f"d{queue[0].upper()}{queue[-1]}{i}" for i in range(n or self.NDMA)]
        for nm in names:
            if nm not in self.sems:
                self.sem(nm)
        self.rr[queue] = [names, 0, engine or queue]

    def dma(self, queue, out_ap, in_ap, waits=(), transpose=False):
        names, idx, engine = self.rr[queue]
        s = names[idx % len(names)]
        self.rr[queue][1] = idx + 1
        prev = self.cnt[s]
        if transpose:
            fn = lambda e, o=out_ap, i=in_ap: e.dma_start(out=o, in_=i, transpose=True)
        else:
            fn = lambda e, o=out_ap, i=in_ap: e.dma_start(out=o, in_=i)
        v = self.emit(engine, fn, waits=list(waits) + [(s, prev)], inc=(s, 16))
        return (s, v)

    def dma_all_done(self, queue):
        names = self.rr[queue][0]
        return [(n, self.cnt[n]) for n in names if self.cnt[n]]

    def emit(self, eng, fn, waits=(), inc=None):
        w = {}
        wm = self.wmax[eng]
        for s, v in waits:
            if v and v > wm.get(s, 0) and v > w.get(s, 0):
                w[s] = v
        for s, v in w.items():
            wm[s] = v
        incv = None
        ret = None
        if inc:
            s, by = inc
            self.cnt[s] += by
            incv = (s, self.cnt[s], by)
            ret = self.cnt[s]
        sems = self.sems

        def op(e, fn=fn, w=tuple(w.items()), incv=incv):
            for s, v in w:
                e.wait_ge(sems[s], v)
            r = fn(e) if fn is not None else None
            if incv is not None:
                assert r is not None, "inc requested but fn returned no instruction"
                r.then_inc(sems[incv[0]], incv[2])

        self.ops[eng].append(op)
        return ret


class Trk:

    def __init__(self):
        self.w = {}
        self.r = {}

    def wrote(self, sem, val):
        self.w[sem] = max(self.w.get(sem, 0), val)

    def read(self, sem, val):
        self.r[sem] = max(self.r.get(sem, 0), val)

    def raw(self):
        return list(self.w.items())

    def war(self):
        m = dict(self.w)
        for s, v in self.r.items():
            m[s] = max(m.get(s, 0), v)
        return list(m.items())


def build(phase_a=True, phase_b=True, debug=False):
    nc = bass.Bass("TRN2", target_bir_lowering=False)

    def din(name, shape, dt=F32):
        return nc.dram_tensor(name, shape, dt, kind="ExternalInput").ap()

    x_d = din("x", [T, D])
    wtkv_d = din("wtkv", [DL, D])
    wc_d = din("wc", [DC, DL])
    gam_d = din("gam", [DC])
    bet_d = din("bet", [DC])
    wq_d = din("wq", [DC, DC])
    wkv_d = din("wkv", [2 * DC, DC])
    wc2l_d = din("wc2l", [DL, DC])
    wor_d = din("wor", [D, DL])
    bor_d = din("bor", [D])
    w1_d = din("w1", [HID, D])
    w2_d = din("w2", [D, HID])
    w3_d = din("w3", [HID, D])
    cosT_d = din("cosT", [128, T])
    sinT_d = din("sinT", [128, T])

    out_d = nc.dram_tensor("out", [T, D], F32, kind="ExternalOutput").ap()

    def scratch(name, shape, dt, dbg=False):
        kind = "ExternalOutput" if (debug and dbg) else "Internal"
        return nc.dram_tensor(name, shape, dt, kind=kind).ap()

    wtkvn = scratch("wtkvn", [DL, D], BF16)
    wcn = scratch("wcn", [DC, DL], BF16)
    wqn = scratch("wqn", [DC, DC], BF16)
    wkvn = scratch("wkvn", [2 * DC, DC], BF16)
    wc2ln = scratch("wc2ln", [DL, DC], BF16)
    worn = scratch("worn", [D, DL], BF16)
    w1n = scratch("w1n", [HID, D], BF16)
    w3n = scratch("w3n", [HID, D], BF16)
    w2n = scratch("w2n", [D, HID], BF16)
    w1t = scratch("w1t", [D, HID], BF16)
    w3t = scratch("w3t", [D, HID], BF16)
    w2s1 = scratch("w2s1", [HID, D], BF16)
    w2p = scratch("w2p", [HID, D], BF16, dbg=True)
    xmid_d = scratch("xmid", [T, D], F32, dbg=True)
    xln2_d = scratch("xln2", [T, D], BF16, dbg=True)
    dbgXT = scratch("dbgXT", [D, 1024], BF16, dbg=True)
    dbgW1 = scratch("dbgW1", [128, 16 * 256], BF16, dbg=True)
    dbgH = scratch("dbgH", [1024, 1024], BF16, dbg=True)
    dbgO = scratch("dbgO", [1024, D], F32, dbg=True)

    ident_i = nc.inline_tensor(np.eye(128, dtype=ml_dtypes.bfloat16), "ident").ap()
    h128_i = nc.inline_tensor(_sylvester(128).astype(ml_dtypes.bfloat16), "h128").ap()
    hbd_np = np.kron(np.eye(NBT), _sylvester(A_BLK))
    hbd_i = nc.inline_tensor(hbd_np.astype(ml_dtypes.bfloat16), "hbd").ap()
    qi, ki = np.meshgrid(np.arange(128), np.arange(128), indexing="ij")
    mask_np = np.where(ki <= qi, np.float32(0.0), np.float32(-1e30)).astype(np.float32)
    mask_i = nc.inline_tensor(mask_np.T.copy().astype(ml_dtypes.bfloat16), "mask").ap()
    ones_i = nc.inline_tensor(np.ones((1, 128), dtype=ml_dtypes.bfloat16), "ones1").ap()

    p = PB(nc)
    for s in ("sT", "sV", "sC", "sP"):
        p.sem(s)
    p.dma_pool("sync")
    p.dma_pool("gpsimd")
    p.dma_pool("s2", engine="sync", n=10)
    p.dma_pool("vdma", engine="gpsimd", n=8)

    st_A = ExitStack()

    def sb(name, shape, dt):
        return st_A.enter_context(nc.sbuf_tensor(name, shape, dt))

    cosg_s = sb("cosg_s", [128, 512], F32)
    sing_s = sb("sing_s", [128, 512], F32)
    ident_s = sb("ident_s", [128, 128], BF16)
    h128_s = sb("h128_s", [128, 128], BF16)
    hbd_s = sb("hbd_s", [128, 128], BF16)
    mask_s = sb("mask_s", [128, 128], BF16)
    wtkvT_s = sb("wtkvT_s", [128, 16 * DL], BF16)
    wcT_s = sb("wcT_s", [128, 4 * DC], BF16)
    wqT_s = sb("wqT_s", [128, 2 * DC], BF16)
    wkvT_s = sb("wkvT_s", [128, 2 * 2 * DC], BF16)
    wc2lT_s = sb("wc2lT_s", [128, 2 * DL], BF16)
    wcomb_s = sb("wcomb_s", [128, 4 * D], BF16)
    gam_s = sb("gam_s", [128, 2], F32)
    bet_s = sb("bet_s", [128, 2], F32)
    betbf_s = sb("betbf_s", [128, 2], BF16)
    bq_s = sb("bq_s", [128, 2], F32)
    bkv_s = sb("bkv_s", [128, 4], F32)
    KT_s = sb("KT_s", [128, 2 * T], BF16)
    VST = DC + 16
    V_s = sb("V_s", [128, NT * VST], BF16)
    xa_b = [sb(f"xa{i}", [128, D], F32) for i in range(2)]
    xlnbf_s = sb("xlnbf_s", [128, D], BF16)
    xlnT_s = sb("xlnT_s", [128, 16 * 512], BF16)
    kvlT_s = sb("kvlT_s", [128, 4 * 512], BF16)
    c_f32 = sb("c_f32", [128, DC], F32)
    cnbf_s = sb("cnbf_s", [128, DC], BF16)
    cnT_s = sb("cnT_s", [128, 2 * 512], BF16)
    qb_s = sb("qb_s", [128, 2 * 512], BF16)
    ropet0 = sb("ropet0", [128, 512], F32)
    ropet1 = sb("ropet1", [128, 512], F32)
    qT_s = sb("qT_s", [128, 2 * 512], BF16)
    vbf_s = sb("vbf_s", [128, 512], BF16)
    o_s = sb("o_s", [128, 4 * DC], BF16)
    oT_s = sb("oT_s", [128, 2 * 512], BF16)
    zT_s = sb("zT_s", [128, 4 * 512], BF16)
    xmid_s = sb("xmid_s", [128, D], F32)
    stat_s = sb("stat_s", [128, 64], F32)
    strip_b = [sb(f"strip{i}", [128, 2048], BF16) for i in range(2)]
    s1i = [sb(f"s1i{i}", [128, D], BF16) for i in range(2)]
    s1o = [sb(f"s1o{i}", [128, D], BF16) for i in range(2)]
    onesb_s = sb("onesb_s", [1, 128], BF16)
    borrow_s = sb("borrow_s", [1, D], F32)
    borbf_s = sb("borbf_s", [1, D], BF16)

    st_psA = ExitStack()
    p_bf = st_psA.enter_context(nc.psum_tensor("p_bf", [128, 1024], BF16))
    pS0 = st_psA.enter_context(nc.psum_tensor("pS0", [128, 512], F32))
    pS1 = st_psA.enter_context(nc.psum_tensor("pS1", [128, 512], F32))
    pX = st_psA.enter_context(nc.psum_tensor("pX", [128, 512], F32))
    pO = st_psA.enter_context(nc.psum_tensor("pO", [128, 2048], F32))

    tk = {}
    for nm in ("xa0", "xa1", "xlnbf", "xlnT", "kvlT", "c_f32", "cnbf", "cnT", "qb", "rt0",
               "rt1", "qT", "vbf", "o", "oT", "zT", "xmid", "scores",
               "stat", "pS0", "pS1", "pX",
               "strip0", "strip1", "s1i0", "s1i1", "s1o0", "s1o1", "wcomb", "wqT", "wkvT",
               "betbf", "borbf", "KT", "V", "xln2T", "w1a", "w1b", "w3a", "w3b",
               "w2g", "hid", "oacc", "sg", "pG", "pL", "pDa", "pDb", "xbm", "xbm2", "outt",
               "cosg", "sing", "expb0", "expb1"):
        tk[nm] = Trk()
    pbf_slot = [Trk() for _ in range(8)]
    tkpo = [Trk() for _ in range(4)]
    st_trks = {}

    def stt(scol):
        return st_trks.setdefault(scol, Trk())

    def po_war():
        m = {}
        for trk in tkpo:
            for s, v in trk.war():
                m[s] = max(m.get(s, 0), v)
        return list(m.items())

    E = p.emit

    casts = [(wtkvn, wtkv_d), (wcn, wc_d), (wqn, wq_d), (wkvn, wkv_d),
             (wc2ln, wc2l_d), (worn, wor_d)]
    cast_done = {}
    for dst, srca in casts:
        cast_done[dst.tensor.name] = p.dma("gpsimd", dst[:, :], srca[:, :])

    def ld(dst_ap, src_ap, waits=()):
        return p.dma("sync", dst_ap, src_ap, waits=waits)

    ld(ident_s[:, :], ident_i[:, :])
    ld(h128_s[:, :], h128_i[:, :])
    ld(hbd_s[:, :], hbd_i[:, :])
    ld(mask_s[:, :], mask_i[:, :])
    gam2 = gam_d.rearrange("(a b) -> a b", b=1)
    bet2 = bet_d.rearrange("(a b) -> a b", b=1)
    v_gam0 = ld(gam_s[:, 0:1], gam2[0:128])
    ld(gam_s[:, 1:2], gam2[128:256])
    ld(bet_s[:, 0:1], bet2[0:128])
    v_bet = ld(bet_s[:, 1:2], bet2[128:256])

    def ldt(dst_ap, src_ap, waits=()):
        return p.dma("sync", dst_ap, src_ap, waits=waits, transpose=True)

    for dc in range(16):
        v = ldt(wtkvT_s[:, dc * DL:(dc + 1) * DL], wtkvn[0:DL, dc * 128:(dc + 1) * 128],
                waits=[cast_done["wtkvn"]])
    wtkvT_done = v
    for lc in range(4):
        v = ldt(wcT_s[:, lc * DC:(lc + 1) * DC], wcn[0:DC, lc * 128:(lc + 1) * 128],
                waits=[cast_done["wcn"]])
    wcT_done = v
    for cc in range(2):
        v = ldt(wqT_s[:, cc * DC:(cc + 1) * DC], wqn[0:DC, cc * 128:(cc + 1) * 128],
                waits=[cast_done["wqn"]])
    wqT_ld = v
    for cc in range(2):
        v = ldt(wkvT_s[:, cc * 2 * DC:(cc + 1) * 2 * DC], wkvn[0:2 * DC, cc * 128:(cc + 1) * 128],
                waits=[cast_done["wkvn"]])
    wkvT_ld = v
    for cc in range(2):
        v = ldt(wc2lT_s[:, cc * DL:(cc + 1) * DL], wc2ln[0:DL, cc * 128:(cc + 1) * 128],
                waits=[cast_done["wc2ln"]])
    wc2lT_done = v

    wcomb_done = 0
    for lc in range(4):
        xb = xa_b[lc % 2]
        xbn = f"xa{lc % 2}"
        v1 = ldt(wcomb_s[:, lc * D:(lc + 1) * D], worn[0:D, lc * 128:(lc + 1) * 128],
                 waits=[cast_done["worn"]] + tk["wcomb"].war())
        v2 = ld(xb[:, :], wtkv_d[lc * 128:(lc + 1) * 128, :], waits=tk[xbn].war())
        tk[xbn].wrote(*v2)
        vv = E("vector", lambda e, lc=lc, xb=xb: e.tensor_tensor(
            out=wcomb_s[:, lc * D:(lc + 1) * D], in0=wcomb_s[:, lc * D:(lc + 1) * D],
            in1=xb[:, :], op=ALU.add),
            waits=[v1, v2], inc=("sV", 1))
        tk[xbn].read("sV", vv)
        tk["wcomb"].wrote("sV", vv)
        wcomb_done = vv

    v = E("vector", lambda e: e.tensor_copy(betbf_s[:, :], bet_s[:, :]),
          waits=[v_bet], inc=("sV", 1))
    tk["betbf"].wrote("sV", v)

    BE = [None]

    def bias_prep():
        for qc in range(2):
            for cc in range(2):
                v = E("tensor", lambda e, qc=qc, cc=cc: e.matmul(
                    pX[:, qc:qc + 1], wqT_s[:, cc * DC + qc * 128: cc * DC + qc * 128 + 128],
                    betbf_s[:, cc:cc + 1], start=(cc == 0), stop=(cc == 1)),
                    waits=[wqT_ld] + tk["betbf"].raw() + tk["pX"].war(),
                    inc=("sT", 1))
        tk["pX"].wrote("sT", v)
        bq_mm = v
        for kc in range(4):
            for cc in range(2):
                v = E("tensor", lambda e, kc=kc, cc=cc: e.matmul(
                    pX[:, 4 + kc: 5 + kc], wkvT_s[:, cc * 2 * DC + kc * 128: cc * 2 * DC + kc * 128 + 128],
                    betbf_s[:, cc:cc + 1], start=(cc == 0), stop=(cc == 1)),
                    waits=[wkvT_ld] + tk["betbf"].raw(),
                    inc=("sT", 1))
        tk["pX"].wrote("sT", v)
        bkv_mm = v
        v = E("scalar", lambda e: e.copy(bq_s[:, 0:2], pX[:, 0:2]),
              waits=[("sT", bq_mm)], inc=("sC", 1))
        v = E("scalar", lambda e: e.copy(bkv_s[:, 0:4], pX[:, 4:8]),
              waits=[("sT", bkv_mm)], inc=("sC", 1))
        tk["pX"].read("sC", v)
        BE[0] = v
        for cc in range(2):
            v = E("vector", lambda e, cc=cc: e.tensor_scalar(
                out=wqT_s[:, cc * DC:(cc + 1) * DC], in0=wqT_s[:, cc * DC:(cc + 1) * DC],
                scalar1=gam_s[:, cc:cc + 1], scalar2=None, op0=ALU.mult),
                waits=[("sT", bq_mm), wqT_ld], inc=("sV", 1))
        tk["wqT"].wrote("sV", v)
        for cc in range(2):
            v = E("vector", lambda e, cc=cc: e.tensor_scalar(
                out=wkvT_s[:, cc * 2 * DC:(cc + 1) * 2 * DC], in0=wkvT_s[:, cc * 2 * DC:(cc + 1) * 2 * DC],
                scalar1=gam_s[:, cc:cc + 1], scalar2=None, op0=ALU.mult),
                waits=[("sT", bkv_mm), wkvT_ld], inc=("sV", 1))
        tk["wkvT"].wrote("sV", v)

    v_onesb = ld(onesb_s[:, :], ones_i[:, :])
    v_bor = ld(borrow_s[:, :], bor_d.rearrange("(b a) -> b a", b=1))
    v = E("vector", lambda e: e.tensor_copy(borbf_s[:, :], borrow_s[:, :]),
          waits=[v_bor], inc=("sV", 1))
    tk["borbf"].wrote("sV", v)
    for it in range(NT):
        v = E("vector", lambda e, it=it: e.tensor_scalar(
            out=V_s[:, it * VST + DC: it * VST + DC + 1], in0=gam_s[:, 0:1],
            scalar1=0.0, scalar2=1.0, op0=ALU.mult, op1=ALU.add),
            waits=[v_gam0] + tk["V"].war(), inc=("sV", 1))
    tk["V"].wrote("sV", v)

    w2s1_r = w2s1.rearrange("(a b) d -> b a d", b=128)
    w2p_r = w2p.rearrange("(a b) d -> b a d", b=128)
    cast_tok = {}
    s1tok = {}
    ptok = {}
    s1c_n = [0]

    def d_cast(dst, src, c0, c1):
        def go():
            cast_tok[(dst.tensor.name, c0)] = p.dma("gpsimd", dst[:, c0:c1], src[:, c0:c1])
        return (None, go)

    def d_piece_t(wn, dcs, r, pc):
        b = pc % 2
        def go():
            ct = cast_tok[(wn.tensor.name, (dcs // 4) * 512)]
            v1 = p.dma("s2", strip_b[b][:, :],
                       wn[r * 2048:(r + 1) * 2048, dcs * 128:(dcs + 1) * 128],
                       waits=[ct] + tk[f"strip{b}"].war(), transpose=True)
            tk[f"strip{b}"].wrote(*v1)
            ptok[pc] = v1
        return (None, go)

    def d_piece_s(wt, dcs, r, pc):
        b = pc % 2
        def go():
            v2 = p.dma("vdma", wt[dcs * 128:(dcs + 1) * 128, r * 2048:(r + 1) * 2048],
                       strip_b[b][:, :], waits=[ptok[pc]])
            tk[f"strip{b}"].read(*v2)
        return (lambda: pc in ptok, go)

    def d_s1(a):
        ib = a % 2
        def go():
            ct = cast_tok[("w2n", (a // 16) * 2048)]
            vin = p.dma("s2", s1i[ib][:, :], w2n[0:D, a * 128:(a + 1) * 128],
                        waits=[ct] + tk[f"s1i{ib}"].war(), transpose=True)
            tk[f"s1i{ib}"].wrote(*vin)
            s1tok[a] = vin
        return ((lambda: s1c_n[0] >= a - 1 and ("w2n", (a // 16) * 2048) in cast_tok), go)

    def c_s1(a):
        ib = a % 2
        def go():
            vin = s1tok[a]
            for nq in range(4):
                vmm = E("tensor", lambda e, ib=ib, nq=nq: e.matmul(
                    pO[:, nq * 512:(nq + 1) * 512], h128_s[:, :],
                    s1i[ib][:, nq * 512:(nq + 1) * 512],
                    start=True, stop=True),
                    waits=[vin] + po_war(), inc=("sT", 1))
            for trk in tkpo:
                trk.wrote("sT", vmm)
            tk[f"s1i{ib}"].read("sT", vmm)
            vev = E("vector", lambda e, ib=ib: e.tensor_copy(s1o[ib][:, :], pO[:, :]),
                    waits=[("sT", vmm)] + tk[f"s1o{ib}"].war(), inc=("sV", 1))
            for trk in tkpo:
                trk.read("sV", vev)
            tk[f"s1o{ib}"].wrote("sV", vev)
            vst = p.dma("vdma", w2s1[a * 128:(a + 1) * 128, :], s1o[ib][:, :],
                        waits=[("sV", vev)])
            tk[f"s1o{ib}"].read(*vst)
            s1c_n[0] += 1
        return (lambda: a in s1tok, go)

    def c_s2k(k):
        ib = k % 2
        def go():
            gate = p.dma_all_done("vdma") if k == 0 else []
            for j in range(NBT):
                v2 = p.dma("s2", s1i[ib][j * A_BLK:(j + 1) * A_BLK, :], w2s1_r[k * NBT + j],
                           waits=(gate + tk[f"s1i{ib}"].war()) if j == 0 else [])
                tk[f"s1i{ib}"].wrote(*v2)
            for nq in range(4):
                vmm = E("tensor", lambda e, ib=ib, nq=nq: e.matmul(
                    pO[:, nq * 512:(nq + 1) * 512], hbd_s[:, :],
                    s1i[ib][:, nq * 512:(nq + 1) * 512],
                    start=True, stop=True),
                    waits=tk[f"s1i{ib}"].raw() + po_war(), inc=("sT", 1))
            for trk in tkpo:
                trk.wrote("sT", vmm)
            tk[f"s1i{ib}"].read("sT", vmm)
            vev = E("scalar", lambda e, ib=ib: e.activation(
                out=s1o[ib][:, :], in_=pO[:, :], func=AF.Copy, scale=FWHT_SCALE),
                waits=[("sT", vmm)] + tk[f"s1o{ib}"].war(), inc=("sC", 1))
            for trk in tkpo:
                trk.read("sC", vev)
            tk[f"s1o{ib}"].wrote("sC", vev)
            for j in range(NBT):
                vb = p.dma("vdma", w2p_r[k * NBT + j], s1o[ib][j * A_BLK:(j + 1) * A_BLK, :],
                           waits=[("sC", vev)] if j == 0 else [])
                tk[f"s1o{ib}"].read(*vb)
        return (lambda: s1c_n[0] == A_BLK, go)

    DQa = [d_s1(a) for a in range(A_BLK)]
    DQb = []
    DQb += [d_cast(w2n, w2_d, j * 2048, (j + 1) * 2048) for j in range(4)]
    DQb += [d_cast(w1n, w1_d, j * 512, (j + 1) * 512) for j in range(4)]
    DQb += [d_cast(w3n, w3_d, j * 512, (j + 1) * 512) for j in range(4)]
    wpieces = []
    pc = 0
    for wn, wt in ((w1n, w1t), (w3n, w3t)):
        for r in range(4):
            for dcs in range(16):
                wpieces.append((d_piece_t(wn, dcs, r, pc), d_piece_s(wt, dcs, r, pc)))
                pc += 1
    for i, (tpt, tps) in enumerate(wpieces):
        DQb.append(tpt)
        if i >= 1:
            DQb.append(wpieces[i - 1][1])
    DQb.append(wpieces[-1][1])
    CQ = [c_s1(a) for a in range(A_BLK)] + [c_s2k(k) for k in range(128 // NBT)]
    dqa_i = [0]
    dqb_i = [0]
    cq_i = [0]

    def run_q(q, qi, cap):
        n = 0
        while qi[0] < len(q) and n < cap:
            rdy, fn = q[qi[0]]
            if rdy is not None and not rdy():
                break
            fn()
            qi[0] += 1
            n += 1

    def dslot():
        run_q(DQb, dqb_i, 5)
        run_q(DQa, dqa_i, 3)

    def cslot():
        dslot()
        run_q(CQ, cq_i, 6)
        run_q(DQa, dqa_i, 3)

    prep_end = [(s, v) for s, v in p.cnt.items() if v]

    st_A2 = ExitStack()
    scores_s = st_A2.enter_context(nc.sbuf_tensor("scores_s", [128, D], F32))
    expb = [st_A2.enter_context(nc.sbuf_tensor(f"expb{i}", [128, 512], BF16)) for i in range(2)]
    for s, v in prep_end:
        tk["scores"].read(s, v)
        tk["expb0"].read(s, v)
        tk["expb1"].read(s, v)

    xmid_store = [None] * NT
    xln2_store = [None] * NT

    def pbf_transpose(src_ap, dst_ap, slot, src_ready, evac_eng="vector"):
        trk = pbf_slot[slot]
        sl = p_bf[:, slot * 128:(slot + 1) * 128]
        vt = E("tensor", lambda e, s=src_ap, sl=sl: e.transpose(sl, s, ident_s[:, :]),
               waits=src_ready + trk.war(), inc=("sT", 1))
        trk.wrote("sT", vt)
        sem = "sV" if evac_eng == "vector" else "sC"
        if evac_eng == "vector":
            vv = E(evac_eng, lambda e, d=dst_ap, sl=sl: e.tensor_copy(d, sl),
                   waits=[("sT", vt)], inc=(sem, 1))
        else:
            vv = E(evac_eng, lambda e, d=dst_ap, sl=sl: e.copy(d, sl),
                   waits=[("sT", vt)], inc=(sem, 1))
        trk.read(sem, vv)
        return sem, vv

    def layer_norm(src_ap, junk_ap, junk_trk, src_waits, out_ap, out_trk, scol,
                   nsplit=1):
        Dn = src_ap.shape[-1]
        st = stat_s
        trk = stt(scol)
        w_st = trk.war()
        v1 = E("vector", lambda e: e.reduce_sum(st[:, scol:scol + 1], src_ap, axis=AX.X,
                                                negate=True),
               waits=src_waits + w_st, inc=("sV", 1))
        v2 = E("scalar", lambda e: e.activation(
            out=junk_ap, in_=src_ap, func=AF.Square,
            accum_out=st[:, scol + 1:scol + 2]),
            waits=src_waits + w_st + junk_trk.war(), inc=("sC", 1))
        junk_trk.wrote("sC", v2)
        v3 = E("vector", lambda e: e.tensor_scalar(
            out=st[:, scol + 2:scol + 4], in0=st[:, scol:scol + 2], scalar1=1.0 / Dn,
            scalar2=None, op0=ALU.mult), waits=[("sV", v1), ("sC", v2)], inc=("sV", 1))
        v5 = E("vector", lambda e: e.scalar_tensor_tensor(
            out=st[:, scol + 4:scol + 5], in0=st[:, scol + 2:scol + 3],
            scalar=st[:, scol + 2:scol + 3], in1=st[:, scol + 3:scol + 4],
            op0=ALU.mult, op1=ALU.subtract),
            waits=[("sV", v3)], inc=("sV", 1))
        v6 = E("vector", lambda e: e.tensor_scalar(
            out=st[:, scol + 4:scol + 5], in0=st[:, scol + 4:scol + 5], scalar1=-1.0,
            scalar2=LN_EPS, op0=ALU.mult, op1=ALU.add), waits=[("sV", v5)], inc=("sV", 1))
        v7 = E("scalar", lambda e: e.activation(
            out=st[:, scol + 5:scol + 6], in_=st[:, scol + 4:scol + 5], func=AF.Ln),
            waits=[("sV", v6)], inc=("sC", 1))
        v8 = E("scalar", lambda e: e.activation(
            out=st[:, scol + 5:scol + 6], in_=st[:, scol + 5:scol + 6], func=AF.Exp,
            scale=-0.5), waits=[("sC", v7)], inc=("sC", 1))
        toks = []
        W = Dn // nsplit
        for s_ in range(nsplit):
            v10 = E("vector", lambda e, s_=s_: e.tensor_scalar(
                out=out_ap[:, s_ * W:(s_ + 1) * W], in0=src_ap[:, s_ * W:(s_ + 1) * W],
                scalar1=st[:, scol + 2:scol + 3],
                scalar2=st[:, scol + 5:scol + 6], op0=ALU.add, op1=ALU.mult),
                waits=[("sC", v8)] + src_waits + out_trk.war(), inc=("sV", 1))
            toks.append(v10)
        out_trk.wrote("sV", toks[-1])
        trk.wrote("sC", v8)
        trk.read("sV", toks[-1])
        trk.read("sV", v3)
        return toks if nsplit > 1 else toks[0]

    if phase_a:
        def a1_tile(g, t):
            it = g * 4 + t
            xb, xbn = xa_b[t % 2], f"xa{t % 2}"
            vx = p.dma("sync", xb[:, :], x_d[it * 128:(it + 1) * 128, :],
                       waits=tk[xbn].war())
            tk[xbn].wrote(*vx)
            vlns = layer_norm(xb[:, :], xmid_s[:, :], tk["xmid"],
                              [vx], xlnbf_s[:, :], tk["xlnbf"], (t % 2) * 8,
                              nsplit=4)
            tk[xbn].read("sV", vlns[-1])
            dslot()
            for dc in range(16):
                sem, vv = pbf_transpose(
                    xlnbf_s[:, dc * 128:(dc + 1) * 128],
                    xlnT_s[:, dc * 512 + t * 128: dc * 512 + t * 128 + 128],
                    (t * 16 + dc) % 8,
                    [("sV", vlns[dc // 4])] + tk["xlnT"].war(),
                    evac_eng=("vector" if dc % 2 == 0 else "scalar"))
                tk["xlnbf"].read("sT", p.cnt["sT"])
                tk["xlnT"].wrote(sem, vv)

        def a1_kvlat(g):
            for lc in range(4):
                for dc in range(16):
                    vmm = E("tensor", lambda e, lc=lc, dc=dc: e.matmul(
                        pX[:, :], wtkvT_s[:, dc * DL + lc * 128: dc * DL + lc * 128 + 128],
                        xlnT_s[:, dc * 512:(dc + 1) * 512],
                        start=(dc == 0), stop=(dc == 15)),
                        waits=(tk["xlnT"].raw() + tk["pX"].war() +
                               ([wtkvT_done] if g == 0 else [])),
                        inc=("sT", 1))
                tk["pX"].wrote("sT", vmm)
                vev = E("scalar", lambda e, lc=lc: e.copy(
                    kvlT_s[:, lc * 512:(lc + 1) * 512], pX[:, :]),
                    waits=[("sT", vmm)] + tk["kvlT"].war(), inc=("sC", 1))
                tk["pX"].read("sC", vev)
                tk["kvlT"].wrote("sC", vev)

        def a1_c(g, t):
            for lc in range(4):
                vmm = E("tensor", lambda e, t=t, lc=lc: e.matmul(
                    pX[:, 0:DC], kvlT_s[:, lc * 512 + t * 128: lc * 512 + t * 128 + 128],
                    wcT_s[:, lc * DC:(lc + 1) * DC],
                    start=(lc == 0), stop=(lc == 3)),
                    waits=(tk["kvlT"].raw() + tk["pX"].war() +
                           ([wcT_done] if g == 0 else [])),
                    inc=("sT", 1))
            tk["pX"].wrote("sT", vmm)
            vev = E("scalar", lambda e: e.copy(c_f32[:, :], pX[:, 0:DC]),
                    waits=[("sT", vmm)] + tk["c_f32"].war(), inc=("sC", 1))
            tk["pX"].read("sC", vev)
            tk["c_f32"].wrote("sC", vev)
            vln = layer_norm(c_f32[:, :], qb_s[:, 0:DC], tk["qb"],
                             [("sC", vev)], cnbf_s[:, :], tk["cnbf"], 16 + (t % 2) * 8)
            tk["c_f32"].read("sV", vln)
            for cc in range(2):
                sem, vv = pbf_transpose(
                    cnbf_s[:, cc * 128:(cc + 1) * 128],
                    cnT_s[:, cc * 512 + t * 128: cc * 512 + t * 128 + 128],
                    (t * 2 + cc) % 8, tk["cnbf"].raw() + tk["cnT"].war(),
                    evac_eng="vector")
                tk["cnbf"].read("sT", p.cnt["sT"])
                tk["cnT"].wrote(sem, vv)

        def a1_pieces(g):
            ps = [lambda t=t: a1_tile(g, t) for t in range(4)]
            ps.append(lambda: a1_kvlat(g))
            ps += [lambda t=t: a1_c(g, t) for t in range(4)]
            return ps

        def a1_qkv_gen(g):
            vcs = p.dma("sync", cosg_s[:, :], cosT_d[:, g * 512:(g + 1) * 512],
                        waits=tk["cosg"].war())
            tk["cosg"].wrote(*vcs)
            vsn = p.dma("sync", sing_s[:, :], sinT_d[:, g * 512:(g + 1) * 512],
                        waits=tk["sing"].war())
            tk["sing"].wrote(*vsn)
            for qc in range(2):
                pb = (pS0, "pS0") if qc % 2 == 0 else (pS1, "pS1")
                for cc in range(2):
                    vmm = E("tensor", lambda e, qc=qc, cc=cc, pb=pb: e.matmul(
                        pb[0][:, :], wqT_s[:, cc * DC + qc * 128: cc * DC + qc * 128 + 128],
                        cnT_s[:, cc * 512:(cc + 1) * 512],
                        start=(cc == 0), stop=(cc == 1)),
                        waits=tk["cnT"].raw() + tk[pb[1]].war() + (tk["wqT"].raw() if g == 0 else []),
                        inc=("sT", 1))
                tk[pb[1]].wrote("sT", vmm)
                vev = E("vector", lambda e, qc=qc, pb=pb: e.tensor_scalar(
                    out=qb_s[:, qc * 512:(qc + 1) * 512], in0=pb[0][:, :],
                    scalar1=bq_s[:, qc:qc + 1], scalar2=None, op0=ALU.add),
                    waits=[("sT", vmm), ("sC", BE[0])] + tk["qb"].war(), inc=("sV", 1))
                tk[pb[1]].read("sV", vev)
                tk["qb"].wrote("sV", vev)
            cosg = cosg_s[:, :]
            sing = sing_s[:, :]
            v1 = E("vector", lambda e: e.tensor_tensor(
                out=ropet0[:, :], in0=qb_s[:, 0:512], in1=cosg, op=ALU.mult),
                waits=tk["qb"].raw() + tk["rt0"].war() + [vcs, vsn], inc=("sV", 1))
            tk["rt0"].wrote("sV", v1)
            v2 = E("vector", lambda e: e.tensor_tensor(
                out=ropet1[:, :], in0=qb_s[:, 512:1024], in1=sing, op=ALU.mult),
                waits=tk["qb"].raw() + tk["rt1"].war(), inc=("sV", 1))
            tk["rt1"].wrote("sV", v2)
            v3 = E("vector", lambda e: e.tensor_tensor(
                out=qT_s[:, 0:512], in0=ropet0[:, :], in1=ropet1[:, :], op=ALU.subtract),
                waits=[("sV", v2)] + tk["qT"].war(), inc=("sV", 1))
            v4 = E("vector", lambda e: e.tensor_tensor(
                out=ropet0[:, :], in0=qb_s[:, 512:1024], in1=cosg, op=ALU.mult),
                waits=[("sV", v3)], inc=("sV", 1))
            v5 = E("vector", lambda e: e.tensor_tensor(
                out=ropet1[:, :], in0=qb_s[:, 0:512], in1=sing, op=ALU.mult),
                waits=[("sV", v4)], inc=("sV", 1))
            v6 = E("vector", lambda e: e.tensor_tensor(
                out=qT_s[:, 512:1024], in0=ropet0[:, :], in1=ropet1[:, :], op=ALU.add),
                waits=[("sV", v5)], inc=("sV", 1))
            tk["qT"].wrote("sV", v6)
            tk["qb"].read("sV", v5)
            tk["rt0"].wrote("sV", v4)
            tk["rt1"].wrote("sV", v5)
            yield
            for kc in range(2):
                pb = (pS0, "pS0") if kc % 2 == 0 else (pS1, "pS1")
                for cc in range(2):
                    vmm = E("tensor", lambda e, kc=kc, cc=cc, pb=pb: e.matmul(
                        pb[0][:, :], wkvT_s[:, cc * 2 * DC + kc * 128: cc * 2 * DC + kc * 128 + 128],
                        cnT_s[:, cc * 512:(cc + 1) * 512],
                        start=(cc == 0), stop=(cc == 1)),
                        waits=tk["cnT"].raw() + tk[pb[1]].war() + (tk["wkvT"].raw() if g == 0 else []),
                        inc=("sT", 1))
                tk[pb[1]].wrote("sT", vmm)
                vev = E("vector", lambda e, kc=kc, pb=pb: e.tensor_scalar(
                    out=qb_s[:, kc * 512:(kc + 1) * 512], in0=pb[0][:, :],
                    scalar1=bkv_s[:, kc:kc + 1], scalar2=None, op0=ALU.add),
                    waits=[("sT", vmm), ("sC", BE[0])] + tk["qb"].war(), inc=("sV", 1))
                tk[pb[1]].read("sV", vev)
                tk["qb"].wrote("sV", vev)
            kdst0 = KT_s[:, 0 * T + g * 512: 0 * T + g * 512 + 512]
            kdst1 = KT_s[:, 1 * T + g * 512: 1 * T + g * 512 + 512]
            v1 = E("vector", lambda e: e.tensor_tensor(
                out=ropet0[:, :], in0=qb_s[:, 0:512], in1=cosg, op=ALU.mult),
                waits=tk["qb"].raw() + tk["rt0"].war(), inc=("sV", 1))
            v2 = E("vector", lambda e: e.tensor_tensor(
                out=ropet1[:, :], in0=qb_s[:, 512:1024], in1=sing, op=ALU.mult),
                waits=tk["qb"].raw() + tk["rt1"].war(), inc=("sV", 1))
            v3 = E("vector", lambda e, kdst0=kdst0: e.tensor_tensor(
                out=kdst0, in0=ropet0[:, :], in1=ropet1[:, :], op=ALU.subtract),
                waits=[("sV", v2)] + tk["KT"].war(), inc=("sV", 1))
            v4 = E("vector", lambda e: e.tensor_tensor(
                out=ropet0[:, :], in0=qb_s[:, 512:1024], in1=cosg, op=ALU.mult),
                waits=[("sV", v3)], inc=("sV", 1))
            v5 = E("vector", lambda e: e.tensor_tensor(
                out=ropet1[:, :], in0=qb_s[:, 0:512], in1=sing, op=ALU.mult),
                waits=[("sV", v4)], inc=("sV", 1))
            v6 = E("vector", lambda e, kdst1=kdst1: e.tensor_tensor(
                out=kdst1, in0=ropet0[:, :], in1=ropet1[:, :], op=ALU.add),
                waits=[("sV", v5)], inc=("sV", 1))
            tk["KT"].wrote("sV", v6)
            tk["qb"].read("sV", v5)
            tk["rt0"].wrote("sV", v4)
            tk["rt1"].wrote("sV", v5)
            tk["cosg"].read("sV", v6)
            tk["sing"].read("sV", v6)
            yield
            for kc in (2, 3):
                pb = (pS0, "pS0") if kc % 2 == 0 else (pS1, "pS1")
                for cc in range(2):
                    vmm = E("tensor", lambda e, kc=kc, cc=cc, pb=pb: e.matmul(
                        pb[0][:, :], wkvT_s[:, cc * 2 * DC + kc * 128: cc * 2 * DC + kc * 128 + 128],
                        cnT_s[:, cc * 512:(cc + 1) * 512],
                        start=(cc == 0), stop=(cc == 1)),
                        waits=tk["cnT"].raw() + tk[pb[1]].war() + (tk["wkvT"].raw() if g == 0 else []),
                        inc=("sT", 1))
                tk[pb[1]].wrote("sT", vmm)
                vev = E("vector", lambda e, kc=kc, pb=pb: e.tensor_scalar(
                    out=vbf_s[:, :], in0=pb[0][:, :],
                    scalar1=bkv_s[:, kc:kc + 1], scalar2=None, op0=ALU.add),
                    waits=[("sT", vmm), ("sC", BE[0])] + tk["vbf"].war(), inc=("sV", 1))
                tk[pb[1]].read("sV", vev)
                tk["vbf"].wrote("sV", vev)
                dvc = kc - 2
                for t in range(4):
                    it = g * 4 + t
                    sem, vv = pbf_transpose(
                        vbf_s[:, t * 128:(t + 1) * 128],
                        V_s[:, it * VST + dvc * 128: it * VST + dvc * 128 + 128],
                        (4 + t) % 8, tk["vbf"].raw() + tk["V"].war(), evac_eng="vector")
                    tk["vbf"].read("sT", p.cnt["sT"])
                    tk["V"].wrote(sem, vv)
                yield

        def emit_scores_exp(g, kb):
            b = kb % 2
            ps, psn = (pS0, "pS0") if kb % 2 == 0 else (pS1, "pS1")
            for cc in range(2):
                vmm = E("tensor", lambda e, kb=kb, cc=cc, ps=ps: e.matmul(
                    ps[:, :], KT_s[:, cc * T + kb * 128: cc * T + kb * 128 + 128],
                    qT_s[:, cc * 512:(cc + 1) * 512],
                    start=(cc == 0), stop=(cc == 1)),
                    waits=tk["qT"].raw() + tk["KT"].raw() + tk[psn].war(),
                    inc=("sT", 1))
            tk[psn].wrote("sT", vmm)
            cut = max(0, kb - g * 4) * 128
            if kb >= g * 4:
                t0 = kb - g * 4
                vm = E("vector", lambda e, t0=t0, ps=ps: e.tensor_tensor(
                    out=scores_s[:, 0:128], in0=ps[:, t0 * 128:(t0 + 1) * 128],
                    in1=mask_s[:, :], op=ALU.add),
                    waits=[("sT", vmm)] + tk["scores"].war(), inc=("sV", 1))
                tk[psn].read("sV", vm)
                tk["scores"].wrote("sV", vm)
                vex = E("scalar", lambda e, b=b, t0=t0: e.activation(
                    out=expb[b][:, t0 * 128:(t0 + 1) * 128], in_=scores_s[:, 0:128],
                    func=AF.Exp, scale=SCALE),
                    waits=[("sV", vm)] + tk[f"expb{b}"].war(), inc=("sC", 1))
                tk["scores"].read("sC", vex)
                tk[f"expb{b}"].wrote("sC", vex)
                if t0 < 3:
                    vex = E("scalar", lambda e, b=b, t0=t0, ps=ps: e.activation(
                        out=expb[b][:, (t0 + 1) * 128:512], in_=ps[:, (t0 + 1) * 128:512],
                        func=AF.Exp, scale=SCALE),
                        waits=[("sT", vmm)], inc=("sC", 1))
                    tk[psn].read("sC", vex)
                    tk[f"expb{b}"].wrote("sC", vex)
            else:
                vex = E("scalar", lambda e, b=b, ps=ps: e.activation(
                    out=expb[b][:, :], in_=ps[:, :], func=AF.Exp, scale=SCALE),
                    waits=[("sT", vmm)] + tk[f"expb{b}"].war(), inc=("sC", 1))
                tk[psn].read("sC", vex)
                tk[f"expb{b}"].wrote("sC", vex)
            return kb, vex, cut

        def emit_avs(g, st):
            kb, vex, cut = st
            b = kb % 2
            for t in range(cut // 128, 4):
                it = g * 4 + t
                vav = E("tensor", lambda e, b=b, t=t, kb=kb: e.matmul(
                    pO[:, t * 512: t * 512 + DC + 1], expb[b][:, t * 128:(t + 1) * 128],
                    V_s[:, kb * VST: kb * VST + DC + 1],
                    start=(kb == 0), stop=(kb == it), skip_group_check=True),
                    waits=[("sC", vex)] + tk["V"].raw() +
                          (tkpo[t].war() if kb == 0 else []),
                    inc=("sT", 1))
                tk[f"expb{b}"].read("sT", vav)
                if kb == it:
                    tkpo[t].wrote("sT", vav)
                    vrc = E("vector", lambda e, t=t: e.reciprocal(
                        stat_s[:, 48 + t:49 + t], pO[:, t * 512 + DC: t * 512 + DC + 1]),
                        waits=[("sT", vav)] + stt(48 + t).war(), inc=("sV", 1))
                    tkpo[t].read("sV", vrc)
                    stt(48 + t).wrote("sV", vrc)
                    vo = E("scalar", lambda e, t=t: e.activation(
                        out=o_s[:, t * DC:(t + 1) * DC], in_=pO[:, t * 512: t * 512 + DC],
                        func=AF.Copy, scale=stat_s[:, 48 + t:49 + t]),
                        waits=[("sV", vrc)] + tk["o"].war(), inc=("sC", 1))
                    tkpo[t].read("sC", vo)
                    tk["o"].wrote("sC", vo)
                    stt(48 + t).read("sC", vo)

        def outproj_tile(g, t):
            it = g * 4 + t
            xb, xbn = xa_b[t % 2], f"xa{t % 2}"
            for nq in range(4):
                for lc in range(4):
                    vmm = E("tensor", lambda e, t=t, nq=nq, lc=lc: e.matmul(
                        pO[:, nq * 512:(nq + 1) * 512],
                        zT_s[:, lc * 512 + t * 128: lc * 512 + t * 128 + 128],
                        wcomb_s[:, lc * D + nq * 512: lc * D + nq * 512 + 512],
                        start=(lc == 0), stop=False),
                        waits=(tk["zT"].raw() +
                               (tkpo[nq].war() if lc == 0 else []) +
                               ([("sV", wcomb_done)] if g == 0 else [])),
                        inc=("sT", 1))
                vmm = E("tensor", lambda e, nq=nq: e.matmul(
                    pO[:, nq * 512:(nq + 1) * 512], onesb_s[0:1, 0:128],
                    borbf_s[0:1, nq * 512:(nq + 1) * 512],
                    start=False, stop=True),
                    waits=([v_onesb] + tk["borbf"].raw()) if g == 0 else [],
                    inc=("sT", 1))
                tkpo[nq].wrote("sT", vmm)
            vx = p.dma("sync", xb[:, :], x_d[it * 128:(it + 1) * 128, :],
                       waits=tk[xbn].war())
            tk[xbn].wrote(*vx)
            vadd = E("vector", lambda e, xb=xb: e.tensor_tensor(
                out=xmid_s[:, :], in0=pO[:, :], in1=xb[:, :], op=ALU.add),
                waits=[("sT", vmm), vx] + tk["xmid"].war(), inc=("sV", 1))
            for trk in tkpo:
                trk.read("sV", vadd)
            tk[xbn].read("sV", vadd)
            tk["xmid"].wrote("sV", vadd)
            vln = layer_norm(xmid_s[:, :], scores_s[:, 0:D], tk["scores"],
                             [("sV", vadd)], xlnbf_s[:, :], tk["xlnbf"], 32 + (t % 2) * 8)
            tk["xmid"].read("sV", vln)
            vst1 = p.dma("gpsimd", xmid_d[it * 128:(it + 1) * 128, :], xmid_s[:, :],
                         waits=[("sV", vln)])
            tk["xmid"].read(*vst1)
            xmid_store[it] = vst1
            vst2 = p.dma("gpsimd", xln2_d[it * 128:(it + 1) * 128, :], xlnbf_s[:, :],
                         waits=[("sV", vln)])
            tk["xlnbf"].read(*vst2)
            xln2_store[it] = vst2
            cslot()

        for f in a1_pieces(0):
            f()
        bias_prep()
        for _ in a1_qkv_gen(0):
            pass
        for g in range(NG):
            nxt = a1_pieces(g + 1) if g + 1 < NG else []
            ni = [0]

            def next_a1(nxt=nxt, ni=ni):
                if ni[0] < len(nxt):
                    nxt[ni[0]]()
                    ni[0] += 1

            nkb = g * 4 + 4
            npc = len(nxt)
            av_pending = None
            for kb in range(nkb):
                st_kb = emit_scores_exp(g, kb)
                if av_pending is not None:
                    emit_avs(g, av_pending)
                av_pending = st_kb
                while ni[0] < npc and ni[0] * nkb <= kb * npc:
                    next_a1()
            emit_avs(g, av_pending)
            for t in range(4):
                for dvc in range(2):
                    sem, vv = pbf_transpose(
                        o_s[:, t * DC + dvc * 128: t * DC + dvc * 128 + 128],
                        oT_s[:, dvc * 512 + t * 128: dvc * 512 + t * 128 + 128],
                        (t * 2 + dvc) % 8, tk["o"].raw() + tk["oT"].war(), evac_eng="vector")
                    tk["o"].read("sT", p.cnt["sT"])
                    tk["oT"].wrote(sem, vv)
            next_a1()
            for lc in range(4):
                pb = (pS0, "pS0") if lc % 2 == 0 else (pS1, "pS1")
                for dvc in range(2):
                    vmm = E("tensor", lambda e, lc=lc, dvc=dvc, pb=pb: e.matmul(
                        pb[0][:, :], wc2lT_s[:, dvc * DL + lc * 128: dvc * DL + lc * 128 + 128],
                        oT_s[:, dvc * 512:(dvc + 1) * 512],
                        start=(dvc == 0), stop=(dvc == 1)),
                        waits=(tk["oT"].raw() + tk[pb[1]].war() +
                               ([wc2lT_done] if g == 0 else [])),
                        inc=("sT", 1))
                tk[pb[1]].wrote("sT", vmm)
                vev = E("scalar", lambda e, lc=lc, pb=pb: e.copy(
                    zT_s[:, lc * 512:(lc + 1) * 512], pb[0][:, :]),
                    waits=[("sT", vmm)] + tk["zT"].war(), inc=("sC", 1))
                tk[pb[1]].read("sC", vev)
                tk["zT"].wrote("sC", vev)
            next_a1()
            while ni[0] < npc:
                next_a1()
            qg = a1_qkv_gen(g + 1) if g + 1 < NG else None
            for t in range(4):
                outproj_tile(g, t)
                if qg is not None:
                    next(qg, None)
            if qg is not None:
                for _ in qg:
                    pass
        run_q(DQa, dqa_i, 10 ** 9)
        run_q(DQb, dqb_i, 10 ** 9)
        run_q(CQ, cq_i, 10 ** 9)
    w13t_all = p.dma_all_done("vdma")
    w2p_all = p.dma_all_done("vdma")

    st_A2.close()
    st_psA.close()
    st_A.close()

    A_end = [(s, v) for s, v in p.cnt.items() if v]

    if phase_b:
        st_B = ExitStack()
        xln2T_s = st_B.enter_context(nc.sbuf_tensor("xln2T_s", [128, 16 * C], BF16))
        oacc = [st_B.enter_context(nc.sbuf_tensor(f"oacc{i}", [128, D], F32)) for i in range(C // 128)]
        hid_s = st_B.enter_context(nc.sbuf_tensor("hid_s", [128, (HG // 128) * C], BF16))
        w1sl = [st_B.enter_context(nc.sbuf_tensor(f"w1sl{i}", [128, 16 * 256], BF16)) for i in range(2)]
        w3sl = [st_B.enter_context(nc.sbuf_tensor(f"w3sl{i}", [128, 16 * 256], BF16)) for i in range(2)]
        w2g_s = st_B.enter_context(nc.sbuf_tensor("w2g_s", [128, (HG // 128) * D], BF16))
        sg_s = st_B.enter_context(nc.sbuf_tensor("sg_s", [128, C], F32))
        xbm = [st_B.enter_context(nc.sbuf_tensor(f"xbm{i}", [128, D], F32)) for i in range(2)]

        pG = st_B.enter_context(nc.psum_tensor("pG", [128, C], F32))
        pL = st_B.enter_context(nc.psum_tensor("pL", [128, C], F32))
        pDa = st_B.enter_context(nc.psum_tensor("pDa", [128, 1024], F32))
        pDb = st_B.enter_context(nc.psum_tensor("pDb", [128, 1024], F32))
        for nm in ("xln2T", "w2g", "sg", "xbm", "xbm2", "pG", "pL", "pDa", "pDb"):
            for s, v in A_end:
                tk[nm].read(s, v)

        w1t_r = w1t.rearrange("(dc p) h -> p dc h", p=128)
        w3t_r = w3t.rearrange("(dc p) h -> p dc h", p=128)
        w2p_r2 = w2p.rearrange("(hc p) d -> p hc d", p=128)

        tkoa = [Trk() for _ in range(C // 128)]
        tkhid = [Trk() for _ in range(HG // 128)]
        w1trk = [Trk() for _ in range(2)]
        w3trk = [Trk() for _ in range(2)]
        for trk in tkoa + tkhid + w1trk + w3trk:
            for s, v in A_end:
                trk.read(s, v)

        def load_xln2T(sw):
            c0 = sw * C
            gate = [xln2_store[c0 // 128 + k] for k in range(C // 128)] if phase_a else []
            for dc in range(16):
                vv = p.dma("sync", xln2T_s[:, dc * C:(dc + 1) * C],
                           xln2_d[c0:c0 + C, dc * 128:(dc + 1) * 128],
                           waits=gate + tk["xln2T"].war(), transpose=True)
                tk["xln2T"].wrote(*vv)
            return vv

        xln2T_pending = load_xln2T(0)
        for sw in range(NSW):
            c0 = sw * C
            xmid_tok = [None] * (C // 128)
            xln2T_ready = xln2T_pending
            if debug and sw == 0:
                for dc in range(16):
                    vdbg = p.dma("gpsimd", dbgXT[dc * 128:(dc + 1) * 128, 0:C],
                                 xln2T_s[:, dc * C:(dc + 1) * C], waits=[xln2T_ready])
                    tk["xln2T"].read(*vdbg)

            NHG = HID // HG
            for hg in range(NHG):
                h0 = hg * HG
                NSL = HG // 256
                slice_ready = [None] * NSL

                def load_slice(sl4, first=False):
                    b = sl4 % 2
                    hh = h0 + sl4 * 256
                    va = p.dma("sync", w1sl[b][:, :], w1t_r[:, :, hh:hh + 256],
                               waits=(w13t_all if first else []) + w1trk[b].war())
                    w1trk[b].wrote(*va)
                    vb = p.dma("sync", w3sl[b][:, :], w3t_r[:, :, hh:hh + 256],
                               waits=w3trk[b].war())
                    w3trk[b].wrote(*vb)
                    slice_ready[sl4] = (va, vb)
                    if debug and sw == 0 and hg == 0 and sl4 == 0:
                        vdbg = p.dma("gpsimd", dbgW1[:, :], w1sl[b][:, :], waits=[va])
                        w1trk[b].read(*vdbg)

                load_slice(0, first=(sw == 0 and hg == 0))
                if NSL > 1:
                    load_slice(1)
                vw2 = p.dma("sync", w2g_s[:, :], w2p_r2[:, h0 // 128:(h0 + HG) // 128, :],
                            waits=(w2p_all if (sw == 0 and hg == 0) else []) + tk["w2g"].war())
                tk["w2g"].wrote(*vw2)
                for hc in range(HG // 128):
                    b = (hc // 2) % 2
                    va, vb = slice_ready[hc // 2]
                    col = (hc % 2) * 128
                    for half in range(C // 512):
                        for dc in range(16):
                            vg = E("tensor", lambda e, b=b, col=col, dc=dc, half=half: e.matmul(
                                pG[:, half * 512:(half + 1) * 512],
                                w1sl[b][:, dc * 256 + col: dc * 256 + col + 128],
                                xln2T_s[:, dc * C + half * 512: dc * C + half * 512 + 512],
                                start=(dc == 0), stop=(dc == 15)),
                                waits=([va, xln2T_ready] + tk["pG"].war()) if (dc == 0 and half == 0) else [],
                                inc=("sT", 1))
                    tk["pG"].wrote("sT", vg)
                    w1trk[b].read("sT", vg)
                    for half in range(C // 512):
                        for dc in range(16):
                            vl = E("tensor", lambda e, b=b, col=col, dc=dc, half=half: e.matmul(
                                pL[:, half * 512:(half + 1) * 512],
                                w3sl[b][:, dc * 256 + col: dc * 256 + col + 128],
                                xln2T_s[:, dc * C + half * 512: dc * C + half * 512 + 512],
                                start=(dc == 0), stop=(dc == 15)),
                                waits=([vb] + tk["pL"].war()) if (dc == 0 and half == 0) else [],
                                inc=("sT", 1))
                    tk["pL"].wrote("sT", vl)
                    w3trk[b].read("sT", vl)
                    tk["xln2T"].read("sT", vl)
                    if hc % 2 == 1 and hc // 2 + 2 < NSL:
                        load_slice(hc // 2 + 2)
                    vsi = E("scalar", lambda e: e.activation(
                        out=sg_s[:, :], in_=pG[:, :], func=AF.Silu),
                        waits=[("sT", vg)] + tk["sg"].war(), inc=("sC", 1))
                    tk["pG"].read("sC", vsi)
                    tk["sg"].wrote("sC", vsi)
                    hslot = hc
                    vmu = E("vector", lambda e, hslot=hslot: e.tensor_tensor(
                        out=hid_s[:, hslot * C:(hslot + 1) * C], in0=sg_s[:, :],
                        in1=pL[:, :], op=ALU.mult),
                        waits=[("sC", vsi), ("sT", vl)] + tkhid[hslot].war(), inc=("sV", 1))
                    tk["pL"].read("sV", vmu)
                    tk["sg"].read("sV", vmu)
                    tkhid[hslot].wrote("sV", vmu)
                    if debug and sw == 0 and hg == 0:
                        vdbg = p.dma("gpsimd", dbgH[hslot * 128:(hslot + 1) * 128, 0:C],
                                     hid_s[:, hslot * C:(hslot + 1) * C],
                                     waits=[("sV", vmu)])
                        tkhid[hslot].read(*vdbg)
                if hg == NHG - 1 and sw + 1 < NSW:
                    xln2T_pending = load_xln2T(sw + 1)

                def load_xmid(tt):
                    it = (c0 // 128) + tt
                    bn = "xbm" if tt % 2 == 0 else "xbm2"
                    vxm = p.dma("sync", xbm[tt % 2][:, :], xmid_d[it * 128:(it + 1) * 128, :],
                                waits=([xmid_store[it]] if phase_a else []) + tk[bn].war())
                    tk[bn].wrote(*vxm)
                    xmid_tok[tt] = vxm

                if hg == 0:
                    load_xmid(0)
                    load_xmid(1)
                hid_ready = [tkhid[i].raw() for i in range(HG // 128)]
                for tt in range(C // 128):
                    if hg == 0 and 1 <= tt and tt + 1 < C // 128:
                        load_xmid(tt + 1)
                    for half in range(2):
                        pD = (pDa, "pDa") if (tt * 2 + half) % 2 == 0 else (pDb, "pDb")
                        for hc in range(HG // 128):
                            for n2 in range(2):
                                nd0 = half * 1024 + n2 * 512
                                vdm = E("tensor", lambda e, tt=tt, hc=hc, n2=n2, nd0=nd0, pD=pD: e.matmul(
                                    pD[0][:, n2 * 512:(n2 + 1) * 512],
                                    hid_s[:, hc * C + tt * 128: hc * C + tt * 128 + 128],
                                    w2g_s[:, hc * D + nd0: hc * D + nd0 + 512],
                                    start=(hc == 0), stop=(hc == HG // 128 - 1)),
                                    waits=(hid_ready[hc] + [vw2] + tk[pD[1]].war()) if (hc == 0 and n2 == 0) else (hid_ready[hc] if n2 == 0 else []),
                                    inc=("sT", 1))
                        tk[pD[1]].wrote("sT", vdm)
                        for hc in range(HG // 128):
                            tkhid[hc].read("sT", vdm)
                        tk["w2g"].read("sT", vdm)
                        if hg == 0:
                            bn = "xbm" if tt % 2 == 0 else "xbm2"
                            vea = E("vector", lambda e, tt=tt, half=half, pD=pD: e.tensor_tensor(
                                out=oacc[tt][:, half * 1024:(half + 1) * 1024],
                                in0=pD[0][:, :], in1=xbm[tt % 2][:, half * 1024:(half + 1) * 1024],
                                op=ALU.add),
                                waits=[("sT", vdm), xmid_tok[tt]] + tkoa[tt].war(), inc=("sV", 1))
                            tk[bn].read("sV", vea)
                        else:
                            vea = E("vector", lambda e, tt=tt, half=half, pD=pD: e.tensor_tensor(
                                out=oacc[tt][:, half * 1024:(half + 1) * 1024],
                                in0=pD[0][:, :], in1=oacc[tt][:, half * 1024:(half + 1) * 1024],
                                op=ALU.add),
                                waits=[("sT", vdm)] + tkoa[tt].war(), inc=("sV", 1))
                        tk[pD[1]].read("sV", vea)
                        tkoa[tt].wrote("sV", vea)
                        if hg == NHG - 1 and half == 1:
                            it = (c0 // 128) + tt
                            vso = p.dma("gpsimd", out_d[it * 128:(it + 1) * 128, :],
                                        oacc[tt][:, :], waits=[("sV", vea)])
                            tkoa[tt].read(*vso)
                        if debug and sw == 0 and hg == 0:
                            vdbg = p.dma("gpsimd",
                                         dbgO[tt * 128:(tt + 1) * 128, half * 1024:(half + 1) * 1024],
                                         oacc[tt][:, half * 1024:(half + 1) * 1024],
                                         waits=[("sV", vea)])
                            tkoa[tt].read(*vdbg)
        st_B.close()

    E("gpsimd", None, waits=p.dma_all_done("gpsimd") + p.dma_all_done("vdma"))
    E("sync", None, waits=p.dma_all_done("sync") + p.dma_all_done("s2"))

    with nc.Block() as block:
        block.sync(lambda e: [f(e) for f in p.ops["sync"]])
        block.scalar(lambda e: [f(e) for f in p.ops["scalar"]])
        block.vector(lambda e: [f(e) for f in p.ops["vector"]])
        block.tensor(lambda e: [f(e) for f in p.ops["tensor"]])
        block.gpsimd(lambda e: [f(e) for f in p.ops["gpsimd"]])

    if phase_b:
        st_B.close()
    return nc


def kernel(**inputs):
    def f32(a):
        return np.ascontiguousarray(np.asarray(a, dtype=np.float32))

    x = f32(inputs["L0_attn_x"])
    cosT, sinT = host_consts()
    common = {
        "wtkv": f32(inputs["L0_attn_W_to_kv"]),
        "wc": f32(inputs["L0_attn_W_compress"]),
        "gam": f32(inputs["L0_attn_ln_gamma"]),
        "bet": f32(inputs["L0_attn_ln_beta"]),
        "wq": f32(inputs["L0_attn_W_q_comp"]),
        "wkv": f32(inputs["L0_attn_W_kv_comp"]),
        "wc2l": f32(inputs["L0_attn_W_comp2lat"]),
        "wor": f32(inputs["L0_attn_W_out_res"]),
        "bor": f32(inputs["L0_attn_b_out_res"]),
        "w1": f32(inputs["L0_ffn_W1"]),
        "w2": f32(inputs["L0_ffn_W2"]),
        "w3": f32(inputs["L0_ffn_W3"]),
        "cosT": cosT,
        "sinT": sinT,
    }
    set_dims()
    nc = build()
    in_maps = [dict(common, x=x[c]) for c in range(8)]
    res = run_bass_kernel_spmd(nc, in_maps, core_ids=list(range(8)))
    return np.stack([r["out"] for r in res.results], axis=0)


if __name__ == "__main__":
    nc = build()
    print("instructions:", nc.n_instructions())


# version 25 (final)
# speedup vs baseline: 1.2008x; 1.0285x over previous
import numpy as np
import ml_dtypes
import concourse.bass as bass
import concourse.mybir as mybir
from concourse.bass_utils import run_bass_kernel_spmd
from contextlib import ExitStack

F32 = mybir.dt.float32
BF16 = mybir.dt.bfloat16
AF = mybir.ActivationFunctionType
ALU = mybir.AluOpType
AX = mybir.AxisListType

D = 2048
DL = 512
DC = 256
LN_EPS = 1e-5
SCALE = 0.0625


def set_dims(t=4096, c=1024, hid=8192, hg=1024):
    global T, NT, NG, C, NSW, HID, HG, A_BLK, NBT, FWHT_SCALE
    T = t
    NT = T // 128
    NG = NT // 4
    C = c
    NSW = T // C
    HID = hid
    HG = hg
    A_BLK = HID // 128
    NBT = 128 // A_BLK
    FWHT_SCALE = float(np.float32(1.0 / np.sqrt(np.float32(HID))))


set_dims()

ENGINES = ("sync", "scalar", "vector", "tensor", "gpsimd")


def _sylvester(n):
    h = np.array([[1.0]], dtype=np.float64)
    while h.shape[0] < n:
        h = np.block([[h, h], [h, -h]])
    return h


def host_consts():
    inv = (1.0 / (10000.0 ** (np.arange(0, DC, 2).astype(np.float32) / np.float32(DC)))).astype(np.float32)
    t = np.arange(T, dtype=np.float32)
    freqs = inv[:, None] * t[None, :]
    cosT = np.cos(freqs).astype(np.float32)
    sinT = np.sin(freqs).astype(np.float32)
    return cosT, sinT


class PB:

    NDMA = 6

    def __init__(self, nc):
        self.nc = nc
        self.ops = {e: [] for e in ENGINES}
        self.cnt = {}
        self.sems = {}
        self.wmax = {e: {} for e in ENGINES}
        self.rr = {}

    def sem(self, name):
        self.sems[name] = self.nc.alloc_semaphore(name)
        self.cnt[name] = 0

    def dma_pool(self, queue, engine=None, n=None):
        names = [f"d{queue[0].upper()}{queue[-1]}{i}" for i in range(n or self.NDMA)]
        for nm in names:
            if nm not in self.sems:
                self.sem(nm)
        self.rr[queue] = [names, 0, engine or queue]

    def dma(self, queue, out_ap, in_ap, waits=(), transpose=False):
        names, idx, engine = self.rr[queue]
        s = names[idx % len(names)]
        self.rr[queue][1] = idx + 1
        prev = self.cnt[s]
        if transpose:
            fn = lambda e, o=out_ap, i=in_ap: e.dma_start(out=o, in_=i, transpose=True)
        else:
            fn = lambda e, o=out_ap, i=in_ap: e.dma_start(out=o, in_=i)
        v = self.emit(engine, fn, waits=list(waits) + [(s, prev)], inc=(s, 16))
        return (s, v)

    def dma_all_done(self, queue):
        names = self.rr[queue][0]
        return [(n, self.cnt[n]) for n in names if self.cnt[n]]

    def emit(self, eng, fn, waits=(), inc=None):
        w = {}
        wm = self.wmax[eng]
        for s, v in waits:
            if v and v > wm.get(s, 0) and v > w.get(s, 0):
                w[s] = v
        for s, v in w.items():
            wm[s] = v
        incv = None
        ret = None
        if inc:
            s, by = inc
            self.cnt[s] += by
            incv = (s, self.cnt[s], by)
            ret = self.cnt[s]
        sems = self.sems

        def op(e, fn=fn, w=tuple(w.items()), incv=incv):
            for s, v in w:
                e.wait_ge(sems[s], v)
            r = fn(e) if fn is not None else None
            if incv is not None:
                assert r is not None, "inc requested but fn returned no instruction"
                r.then_inc(sems[incv[0]], incv[2])

        self.ops[eng].append(op)
        return ret


class Trk:

    def __init__(self):
        self.w = {}
        self.r = {}

    def wrote(self, sem, val):
        self.w[sem] = max(self.w.get(sem, 0), val)

    def read(self, sem, val):
        self.r[sem] = max(self.r.get(sem, 0), val)

    def raw(self):
        return list(self.w.items())

    def war(self):
        m = dict(self.w)
        for s, v in self.r.items():
            m[s] = max(m.get(s, 0), v)
        return list(m.items())


def build(phase_a=True, phase_b=True, debug=False):
    nc = bass.Bass("TRN2", target_bir_lowering=False)

    def din(name, shape, dt=F32):
        return nc.dram_tensor(name, shape, dt, kind="ExternalInput").ap()

    x_d = din("x", [T, D])
    wtkv_d = din("wtkv", [DL, D])
    wc_d = din("wc", [DC, DL])
    gam_d = din("gam", [DC])
    bet_d = din("bet", [DC])
    wq_d = din("wq", [DC, DC])
    wkv_d = din("wkv", [2 * DC, DC])
    wc2l_d = din("wc2l", [DL, DC])
    wor_d = din("wor", [D, DL])
    bor_d = din("bor", [D])
    w1_d = din("w1", [HID, D])
    w2_d = din("w2", [D, HID])
    w3_d = din("w3", [HID, D])
    cosT_d = din("cosT", [128, T])
    sinT_d = din("sinT", [128, T])

    out_d = nc.dram_tensor("out", [T, D], F32, kind="ExternalOutput").ap()

    def scratch(name, shape, dt, dbg=False):
        kind = "ExternalOutput" if (debug and dbg) else "Internal"
        return nc.dram_tensor(name, shape, dt, kind=kind).ap()

    wtkvn = scratch("wtkvn", [DL, D], BF16)
    wcn = scratch("wcn", [DC, DL], BF16)
    wqn = scratch("wqn", [DC, DC], BF16)
    wkvn = scratch("wkvn", [2 * DC, DC], BF16)
    wc2ln = scratch("wc2ln", [DL, DC], BF16)
    worn = scratch("worn", [D, DL], BF16)
    w1n = scratch("w1n", [HID, D], BF16)
    w3n = scratch("w3n", [HID, D], BF16)
    w2n = scratch("w2n", [D, HID], BF16)
    w1t = scratch("w1t", [D, HID], BF16)
    w3t = scratch("w3t", [D, HID], BF16)
    w2s1 = scratch("w2s1", [HID, D], BF16)
    w2p = scratch("w2p", [HID, D], BF16, dbg=True)
    xmid_d = scratch("xmid", [T, D], F32, dbg=True)
    xln2_d = scratch("xln2", [T, D], BF16, dbg=True)
    dbgXT = scratch("dbgXT", [D, 1024], BF16, dbg=True)
    dbgW1 = scratch("dbgW1", [128, 16 * 256], BF16, dbg=True)
    dbgH = scratch("dbgH", [1024, 1024], BF16, dbg=True)
    dbgO = scratch("dbgO", [1024, D], F32, dbg=True)

    ident_i = nc.inline_tensor(np.eye(128, dtype=ml_dtypes.bfloat16), "ident").ap()
    h128_i = nc.inline_tensor(_sylvester(128).astype(ml_dtypes.bfloat16), "h128").ap()
    hbd_np = np.kron(np.eye(NBT), _sylvester(A_BLK))
    hbd_i = nc.inline_tensor(hbd_np.astype(ml_dtypes.bfloat16), "hbd").ap()
    qi, ki = np.meshgrid(np.arange(128), np.arange(128), indexing="ij")
    mask_np = np.where(ki <= qi, np.float32(0.0), np.float32(-1e30)).astype(np.float32)
    mask_i = nc.inline_tensor(mask_np.T.copy().astype(ml_dtypes.bfloat16), "mask").ap()
    ones_i = nc.inline_tensor(np.ones((1, 128), dtype=ml_dtypes.bfloat16), "ones1").ap()

    p = PB(nc)
    for s in ("sT", "sV", "sC", "sP"):
        p.sem(s)
    p.dma_pool("sync")
    p.dma_pool("gpsimd")
    p.dma_pool("s2", engine="sync", n=10)
    p.dma_pool("vdma", engine="sync", n=8)

    st_A = ExitStack()

    def sb(name, shape, dt):
        return st_A.enter_context(nc.sbuf_tensor(name, shape, dt))

    cosg_s = sb("cosg_s", [128, 512], F32)
    sing_s = sb("sing_s", [128, 512], F32)
    ident_s = sb("ident_s", [128, 128], BF16)
    h128_s = sb("h128_s", [128, 128], BF16)
    hbd_s = sb("hbd_s", [128, 128], BF16)
    mask_s = sb("mask_s", [128, 128], BF16)
    wtkvT_s = sb("wtkvT_s", [128, 16 * DL], BF16)
    wcT_s = sb("wcT_s", [128, 4 * DC], BF16)
    wqT_s = sb("wqT_s", [128, 2 * DC], BF16)
    wkvT_s = sb("wkvT_s", [128, 2 * 2 * DC], BF16)
    wc2lT_s = sb("wc2lT_s", [128, 2 * DL], BF16)
    wcomb_s = sb("wcomb_s", [128, 4 * D], BF16)
    gam_s = sb("gam_s", [128, 2], F32)
    bet_s = sb("bet_s", [128, 2], F32)
    betbf_s = sb("betbf_s", [128, 2], BF16)
    bq_s = sb("bq_s", [128, 2], F32)
    bkv_s = sb("bkv_s", [128, 4], F32)
    KT_s = sb("KT_s", [128, 2 * T], BF16)
    VST = DC + 16
    V_s = sb("V_s", [128, NT * VST], BF16)
    xa_b = [sb(f"xa{i}", [128, D], F32) for i in range(2)]
    xlnbf_s = sb("xlnbf_s", [128, D], BF16)
    xlnT_s = sb("xlnT_s", [128, 16 * 512], BF16)
    kvlT_s = sb("kvlT_s", [128, 4 * 512], BF16)
    c_f32 = sb("c_f32", [128, DC], F32)
    cnbf_s = sb("cnbf_s", [128, DC], BF16)
    cnT_s = sb("cnT_s", [128, 2 * 512], BF16)
    qb_s = sb("qb_s", [128, 2 * 512], BF16)
    ropet0 = sb("ropet0", [128, 512], F32)
    ropet1 = sb("ropet1", [128, 512], F32)
    qT_s = sb("qT_s", [128, 2 * 512], BF16)
    vbf_s = sb("vbf_s", [128, 512], BF16)
    o_s = sb("o_s", [128, 4 * DC], BF16)
    oT_s = sb("oT_s", [128, 2 * 512], BF16)
    zT_s = sb("zT_s", [128, 4 * 512], BF16)
    xmid_s = sb("xmid_s", [128, D], F32)
    stat_s = sb("stat_s", [128, 64], F32)
    strip_b = [sb(f"strip{i}", [128, 2048], BF16) for i in range(2)]
    s1i = [sb(f"s1i{i}", [128, D], BF16) for i in range(2)]
    s1o = [sb(f"s1o{i}", [128, D], BF16) for i in range(2)]
    onesb_s = sb("onesb_s", [1, 128], BF16)
    borrow_s = sb("borrow_s", [1, D], F32)
    borbf_s = sb("borbf_s", [1, D], BF16)

    st_psA = ExitStack()
    p_bf = st_psA.enter_context(nc.psum_tensor("p_bf", [128, 1024], BF16))
    pS0 = st_psA.enter_context(nc.psum_tensor("pS0", [128, 512], F32))
    pS1 = st_psA.enter_context(nc.psum_tensor("pS1", [128, 512], F32))
    pX = st_psA.enter_context(nc.psum_tensor("pX", [128, 512], F32))
    pO = st_psA.enter_context(nc.psum_tensor("pO", [128, 2048], F32))

    tk = {}
    for nm in ("xa0", "xa1", "xlnbf", "xlnT", "kvlT", "c_f32", "cnbf", "cnT", "qb", "rt0",
               "rt1", "qT", "vbf", "o", "oT", "zT", "xmid", "scores",
               "stat", "pS0", "pS1", "pX",
               "strip0", "strip1", "s1i0", "s1i1", "s1o0", "s1o1", "wcomb", "wqT", "wkvT",
               "betbf", "borbf", "KT", "V", "xln2T", "w1a", "w1b", "w3a", "w3b",
               "w2g", "hid", "oacc", "sg", "pG", "pL", "pDa", "pDb", "xbm", "xbm2", "outt",
               "cosg", "sing", "expb0", "expb1"):
        tk[nm] = Trk()
    pbf_slot = [Trk() for _ in range(8)]
    tkpo = [Trk() for _ in range(4)]
    st_trks = {}

    def stt(scol):
        return st_trks.setdefault(scol, Trk())

    def po_war():
        m = {}
        for trk in tkpo:
            for s, v in trk.war():
                m[s] = max(m.get(s, 0), v)
        return list(m.items())

    E = p.emit

    casts = [(wtkvn, wtkv_d), (wcn, wc_d), (wqn, wq_d), (wkvn, wkv_d),
             (wc2ln, wc2l_d), (worn, wor_d)]
    cast_done = {}
    for dst, srca in casts:
        cast_done[dst.tensor.name] = p.dma("gpsimd", dst[:, :], srca[:, :])

    def ld(dst_ap, src_ap, waits=()):
        return p.dma("sync", dst_ap, src_ap, waits=waits)

    ld(ident_s[:, :], ident_i[:, :])
    ld(h128_s[:, :], h128_i[:, :])
    ld(hbd_s[:, :], hbd_i[:, :])
    ld(mask_s[:, :], mask_i[:, :])
    gam2 = gam_d.rearrange("(a b) -> a b", b=1)
    bet2 = bet_d.rearrange("(a b) -> a b", b=1)
    v_gam0 = ld(gam_s[:, 0:1], gam2[0:128])
    ld(gam_s[:, 1:2], gam2[128:256])
    ld(bet_s[:, 0:1], bet2[0:128])
    v_bet = ld(bet_s[:, 1:2], bet2[128:256])

    def ldt(dst_ap, src_ap, waits=()):
        return p.dma("sync", dst_ap, src_ap, waits=waits, transpose=True)

    for dc in range(16):
        v = ldt(wtkvT_s[:, dc * DL:(dc + 1) * DL], wtkvn[0:DL, dc * 128:(dc + 1) * 128],
                waits=[cast_done["wtkvn"]])
    wtkvT_done = v
    for lc in range(4):
        v = ldt(wcT_s[:, lc * DC:(lc + 1) * DC], wcn[0:DC, lc * 128:(lc + 1) * 128],
                waits=[cast_done["wcn"]])
    wcT_done = v
    for cc in range(2):
        v = ldt(wqT_s[:, cc * DC:(cc + 1) * DC], wqn[0:DC, cc * 128:(cc + 1) * 128],
                waits=[cast_done["wqn"]])
    wqT_ld = v
    for cc in range(2):
        v = ldt(wkvT_s[:, cc * 2 * DC:(cc + 1) * 2 * DC], wkvn[0:2 * DC, cc * 128:(cc + 1) * 128],
                waits=[cast_done["wkvn"]])
    wkvT_ld = v
    for cc in range(2):
        v = ldt(wc2lT_s[:, cc * DL:(cc + 1) * DL], wc2ln[0:DL, cc * 128:(cc + 1) * 128],
                waits=[cast_done["wc2ln"]])
    wc2lT_done = v

    wcomb_done = 0
    for lc in range(4):
        xb = xa_b[lc % 2]
        xbn = f"xa{lc % 2}"
        v1 = ldt(wcomb_s[:, lc * D:(lc + 1) * D], worn[0:D, lc * 128:(lc + 1) * 128],
                 waits=[cast_done["worn"]] + tk["wcomb"].war())
        v2 = ld(xb[:, :], wtkv_d[lc * 128:(lc + 1) * 128, :], waits=tk[xbn].war())
        tk[xbn].wrote(*v2)
        vv = E("vector", lambda e, lc=lc, xb=xb: e.tensor_tensor(
            out=wcomb_s[:, lc * D:(lc + 1) * D], in0=wcomb_s[:, lc * D:(lc + 1) * D],
            in1=xb[:, :], op=ALU.add),
            waits=[v1, v2], inc=("sV", 1))
        tk[xbn].read("sV", vv)
        tk["wcomb"].wrote("sV", vv)
        wcomb_done = vv

    v = E("vector", lambda e: e.tensor_copy(betbf_s[:, :], bet_s[:, :]),
          waits=[v_bet], inc=("sV", 1))
    tk["betbf"].wrote("sV", v)

    BE = [None]

    def bias_prep():
        for qc in range(2):
            for cc in range(2):
                v = E("tensor", lambda e, qc=qc, cc=cc: e.matmul(
                    pX[:, qc:qc + 1], wqT_s[:, cc * DC + qc * 128: cc * DC + qc * 128 + 128],
                    betbf_s[:, cc:cc + 1], start=(cc == 0), stop=(cc == 1)),
                    waits=[wqT_ld] + tk["betbf"].raw() + tk["pX"].war(),
                    inc=("sT", 1))
        tk["pX"].wrote("sT", v)
        bq_mm = v
        for kc in range(4):
            for cc in range(2):
                v = E("tensor", lambda e, kc=kc, cc=cc: e.matmul(
                    pX[:, 4 + kc: 5 + kc], wkvT_s[:, cc * 2 * DC + kc * 128: cc * 2 * DC + kc * 128 + 128],
                    betbf_s[:, cc:cc + 1], start=(cc == 0), stop=(cc == 1)),
                    waits=[wkvT_ld] + tk["betbf"].raw(),
                    inc=("sT", 1))
        tk["pX"].wrote("sT", v)
        bkv_mm = v
        v = E("scalar", lambda e: e.copy(bq_s[:, 0:2], pX[:, 0:2]),
              waits=[("sT", bq_mm)], inc=("sC", 1))
        v = E("scalar", lambda e: e.copy(bkv_s[:, 0:4], pX[:, 4:8]),
              waits=[("sT", bkv_mm)], inc=("sC", 1))
        tk["pX"].read("sC", v)
        BE[0] = v
        for cc in range(2):
            v = E("vector", lambda e, cc=cc: e.tensor_scalar(
                out=wqT_s[:, cc * DC:(cc + 1) * DC], in0=wqT_s[:, cc * DC:(cc + 1) * DC],
                scalar1=gam_s[:, cc:cc + 1], scalar2=None, op0=ALU.mult),
                waits=[("sT", bq_mm), wqT_ld], inc=("sV", 1))
        tk["wqT"].wrote("sV", v)
        for cc in range(2):
            v = E("vector", lambda e, cc=cc: e.tensor_scalar(
                out=wkvT_s[:, cc * 2 * DC:(cc + 1) * 2 * DC], in0=wkvT_s[:, cc * 2 * DC:(cc + 1) * 2 * DC],
                scalar1=gam_s[:, cc:cc + 1], scalar2=None, op0=ALU.mult),
                waits=[("sT", bkv_mm), wkvT_ld], inc=("sV", 1))
        tk["wkvT"].wrote("sV", v)

    v_onesb = ld(onesb_s[:, :], ones_i[:, :])
    v_bor = ld(borrow_s[:, :], bor_d.rearrange("(b a) -> b a", b=1))
    v = E("vector", lambda e: e.tensor_copy(borbf_s[:, :], borrow_s[:, :]),
          waits=[v_bor], inc=("sV", 1))
    tk["borbf"].wrote("sV", v)
    for it in range(NT):
        v = E("vector", lambda e, it=it: e.tensor_scalar(
            out=V_s[:, it * VST + DC: it * VST + DC + 1], in0=gam_s[:, 0:1],
            scalar1=0.0, scalar2=1.0, op0=ALU.mult, op1=ALU.add),
            waits=[v_gam0] + tk["V"].war(), inc=("sV", 1))
    tk["V"].wrote("sV", v)

    w2s1_r = w2s1.rearrange("(a b) d -> b a d", b=128)
    w2p_r = w2p.rearrange("(a b) d -> b a d", b=128)
    cast_tok = {}
    s1tok = {}
    ptok = {}
    s1c_n = [0]

    def d_cast(dst, src, c0, c1, queue="gpsimd"):
        def go():
            cast_tok[(dst.tensor.name, c0)] = p.dma(queue, dst[:, c0:c1], src[:, c0:c1])
        return (None, go)

    def d_piece_t(wn, dcs, r, pc):
        b = pc % 2
        def go():
            ct = cast_tok[(wn.tensor.name, (dcs // 4) * 512)]
            v1 = p.dma("s2", strip_b[b][:, :],
                       wn[r * 2048:(r + 1) * 2048, dcs * 128:(dcs + 1) * 128],
                       waits=[ct] + tk[f"strip{b}"].war(), transpose=True)
            tk[f"strip{b}"].wrote(*v1)
            ptok[pc] = v1
        return (None, go)

    def d_piece_s(wt, dcs, r, pc):
        b = pc % 2
        def go():
            v2 = p.dma("vdma", wt[dcs * 128:(dcs + 1) * 128, r * 2048:(r + 1) * 2048],
                       strip_b[b][:, :], waits=[ptok[pc]])
            tk[f"strip{b}"].read(*v2)
        return (lambda: pc in ptok, go)

    def d_s1(a):
        ib = a % 2
        def go():
            ct = cast_tok[("w2n", (a // 16) * 2048)]
            vin = p.dma("s2", s1i[ib][:, :], w2n[0:D, a * 128:(a + 1) * 128],
                        waits=[ct] + tk[f"s1i{ib}"].war(), transpose=True)
            tk[f"s1i{ib}"].wrote(*vin)
            s1tok[a] = vin
        return ((lambda: s1c_n[0] >= a - 1 and ("w2n", (a // 16) * 2048) in cast_tok), go)

    def c_s1(a):
        ib = a % 2
        def go():
            vin = s1tok[a]
            for nq in range(4):
                vmm = E("tensor", lambda e, ib=ib, nq=nq: e.matmul(
                    pO[:, nq * 512:(nq + 1) * 512], h128_s[:, :],
                    s1i[ib][:, nq * 512:(nq + 1) * 512],
                    start=True, stop=True),
                    waits=[vin] + po_war(), inc=("sT", 1))
            for trk in tkpo:
                trk.wrote("sT", vmm)
            tk[f"s1i{ib}"].read("sT", vmm)
            vev = E("vector", lambda e, ib=ib: e.tensor_copy(s1o[ib][:, :], pO[:, :]),
                    waits=[("sT", vmm)] + tk[f"s1o{ib}"].war(), inc=("sV", 1))
            for trk in tkpo:
                trk.read("sV", vev)
            tk[f"s1o{ib}"].wrote("sV", vev)
            vst = p.dma("gpsimd", w2s1[a * 128:(a + 1) * 128, :], s1o[ib][:, :],
                        waits=[("sV", vev)])
            tk[f"s1o{ib}"].read(*vst)
            s1c_n[0] += 1
        return (lambda: a in s1tok, go)

    def c_s2k(k):
        ib = k % 2
        def go():
            gate = p.dma_all_done("gpsimd") if k == 0 else []
            for j in range(NBT):
                v2 = p.dma("s2", s1i[ib][j * A_BLK:(j + 1) * A_BLK, :], w2s1_r[k * NBT + j],
                           waits=(gate + tk[f"s1i{ib}"].war()) if j == 0 else [])
                tk[f"s1i{ib}"].wrote(*v2)
            for nq in range(4):
                vmm = E("tensor", lambda e, ib=ib, nq=nq: e.matmul(
                    pO[:, nq * 512:(nq + 1) * 512], hbd_s[:, :],
                    s1i[ib][:, nq * 512:(nq + 1) * 512],
                    start=True, stop=True),
                    waits=tk[f"s1i{ib}"].raw() + po_war(), inc=("sT", 1))
            for trk in tkpo:
                trk.wrote("sT", vmm)
            tk[f"s1i{ib}"].read("sT", vmm)
            vev = E("scalar", lambda e, ib=ib: e.activation(
                out=s1o[ib][:, :], in_=pO[:, :], func=AF.Copy, scale=FWHT_SCALE),
                waits=[("sT", vmm)] + tk[f"s1o{ib}"].war(), inc=("sC", 1))
            for trk in tkpo:
                trk.read("sC", vev)
            tk[f"s1o{ib}"].wrote("sC", vev)
            for j in range(NBT):
                vb = p.dma("gpsimd", w2p_r[k * NBT + j], s1o[ib][j * A_BLK:(j + 1) * A_BLK, :],
                           waits=[("sC", vev)] if j == 0 else [])
                tk[f"s1o{ib}"].read(*vb)
        return (lambda: s1c_n[0] == A_BLK, go)

    DQa = [d_s1(a) for a in range(A_BLK)]
    DQb = []
    DQb += [d_cast(w2n, w2_d, j * 2048, (j + 1) * 2048) for j in range(4)]
    DQb += [d_cast(w1n, w1_d, j * 512, (j + 1) * 512) for j in range(4)]
    DQb += [d_cast(w3n, w3_d, j * 512, (j + 1) * 512) for j in range(4)]
    wpieces = []
    pc = 0
    for wn, wt in ((w1n, w1t), (w3n, w3t)):
        for r in range(4):
            for dcs in range(16):
                wpieces.append((d_piece_t(wn, dcs, r, pc), d_piece_s(wt, dcs, r, pc)))
                pc += 1
    for i, (tpt, tps) in enumerate(wpieces):
        DQb.append(tpt)
        if i >= 1:
            DQb.append(wpieces[i - 1][1])
    DQb.append(wpieces[-1][1])
    CQ = [c_s1(a) for a in range(A_BLK)] + [c_s2k(k) for k in range(128 // NBT)]
    dqa_i = [0]
    dqb_i = [0]
    cq_i = [0]

    def run_q(q, qi, cap):
        n = 0
        while qi[0] < len(q) and n < cap:
            rdy, fn = q[qi[0]]
            if rdy is not None and not rdy():
                break
            fn()
            qi[0] += 1
            n += 1

    def dslot():
        run_q(DQb, dqb_i, 5)
        run_q(DQa, dqa_i, 3)

    def cslot():
        dslot()
        run_q(CQ, cq_i, 6)
        run_q(DQa, dqa_i, 3)

    prep_end = [(s, v) for s, v in p.cnt.items() if v]

    st_A2 = ExitStack()
    scores_s = st_A2.enter_context(nc.sbuf_tensor("scores_s", [128, D], F32))
    expb = [st_A2.enter_context(nc.sbuf_tensor(f"expb{i}", [128, 512], BF16)) for i in range(2)]
    for s, v in prep_end:
        tk["scores"].read(s, v)
        tk["expb0"].read(s, v)
        tk["expb1"].read(s, v)

    xmid_store = [None] * NT
    xln2_store = [None] * NT

    def pbf_transpose(src_ap, dst_ap, slot, src_ready, evac_eng="vector"):
        trk = pbf_slot[slot]
        sl = p_bf[:, slot * 128:(slot + 1) * 128]
        vt = E("tensor", lambda e, s=src_ap, sl=sl: e.transpose(sl, s, ident_s[:, :]),
               waits=src_ready + trk.war(), inc=("sT", 1))
        trk.wrote("sT", vt)
        sem = "sV" if evac_eng == "vector" else "sC"
        if evac_eng == "vector":
            vv = E(evac_eng, lambda e, d=dst_ap, sl=sl: e.tensor_copy(d, sl),
                   waits=[("sT", vt)], inc=(sem, 1))
        else:
            vv = E(evac_eng, lambda e, d=dst_ap, sl=sl: e.copy(d, sl),
                   waits=[("sT", vt)], inc=(sem, 1))
        trk.read(sem, vv)
        return sem, vv

    def layer_norm(src_ap, junk_ap, junk_trk, src_waits, out_ap, out_trk, scol,
                   nsplit=1):
        Dn = src_ap.shape[-1]
        st = stat_s
        trk = stt(scol)
        w_st = trk.war()
        v1 = E("vector", lambda e: e.reduce_sum(st[:, scol:scol + 1], src_ap, axis=AX.X,
                                                negate=True),
               waits=src_waits + w_st, inc=("sV", 1))
        v2 = E("scalar", lambda e: e.activation(
            out=junk_ap, in_=src_ap, func=AF.Square,
            accum_out=st[:, scol + 1:scol + 2]),
            waits=src_waits + w_st + junk_trk.war(), inc=("sC", 1))
        junk_trk.wrote("sC", v2)
        v3 = E("vector", lambda e: e.tensor_scalar(
            out=st[:, scol + 2:scol + 4], in0=st[:, scol:scol + 2], scalar1=1.0 / Dn,
            scalar2=None, op0=ALU.mult), waits=[("sV", v1), ("sC", v2)], inc=("sV", 1))
        v5 = E("vector", lambda e: e.scalar_tensor_tensor(
            out=st[:, scol + 4:scol + 5], in0=st[:, scol + 2:scol + 3],
            scalar=st[:, scol + 2:scol + 3], in1=st[:, scol + 3:scol + 4],
            op0=ALU.mult, op1=ALU.subtract),
            waits=[("sV", v3)], inc=("sV", 1))
        v6 = E("vector", lambda e: e.tensor_scalar(
            out=st[:, scol + 4:scol + 5], in0=st[:, scol + 4:scol + 5], scalar1=-1.0,
            scalar2=LN_EPS, op0=ALU.mult, op1=ALU.add), waits=[("sV", v5)], inc=("sV", 1))
        v7 = E("scalar", lambda e: e.activation(
            out=st[:, scol + 5:scol + 6], in_=st[:, scol + 4:scol + 5], func=AF.Ln),
            waits=[("sV", v6)], inc=("sC", 1))
        v8 = E("scalar", lambda e: e.activation(
            out=st[:, scol + 5:scol + 6], in_=st[:, scol + 5:scol + 6], func=AF.Exp,
            scale=-0.5), waits=[("sC", v7)], inc=("sC", 1))
        toks = []
        W = Dn // nsplit
        for s_ in range(nsplit):
            v10 = E("vector", lambda e, s_=s_: e.tensor_scalar(
                out=out_ap[:, s_ * W:(s_ + 1) * W], in0=src_ap[:, s_ * W:(s_ + 1) * W],
                scalar1=st[:, scol + 2:scol + 3],
                scalar2=st[:, scol + 5:scol + 6], op0=ALU.add, op1=ALU.mult),
                waits=[("sC", v8)] + src_waits + out_trk.war(), inc=("sV", 1))
            toks.append(v10)
        out_trk.wrote("sV", toks[-1])
        trk.wrote("sC", v8)
        trk.read("sV", toks[-1])
        trk.read("sV", v3)
        return toks if nsplit > 1 else toks[0]

    if phase_a:
        def a1_tile(g, t):
            it = g * 4 + t
            xb, xbn = xa_b[t % 2], f"xa{t % 2}"
            vx = p.dma("sync", xb[:, :], x_d[it * 128:(it + 1) * 128, :],
                       waits=tk[xbn].war())
            tk[xbn].wrote(*vx)
            vlns = layer_norm(xb[:, :], xmid_s[:, :], tk["xmid"],
                              [vx], xlnbf_s[:, :], tk["xlnbf"], (t % 2) * 8,
                              nsplit=4)
            tk[xbn].read("sV", vlns[-1])
            dslot()
            for dc in range(16):
                sem, vv = pbf_transpose(
                    xlnbf_s[:, dc * 128:(dc + 1) * 128],
                    xlnT_s[:, dc * 512 + t * 128: dc * 512 + t * 128 + 128],
                    (t * 16 + dc) % 8,
                    [("sV", vlns[dc // 4])] + tk["xlnT"].war(),
                    evac_eng=("vector" if dc % 2 == 0 else "scalar"))
                tk["xlnbf"].read("sT", p.cnt["sT"])
                tk["xlnT"].wrote(sem, vv)

        def a1_kvlat(g):
            for lc in range(4):
                for dc in range(16):
                    vmm = E("tensor", lambda e, lc=lc, dc=dc: e.matmul(
                        pX[:, :], wtkvT_s[:, dc * DL + lc * 128: dc * DL + lc * 128 + 128],
                        xlnT_s[:, dc * 512:(dc + 1) * 512],
                        start=(dc == 0), stop=(dc == 15)),
                        waits=(tk["xlnT"].raw() + tk["pX"].war() +
                               ([wtkvT_done] if g == 0 else [])),
                        inc=("sT", 1))
                tk["pX"].wrote("sT", vmm)
                vev = E("scalar", lambda e, lc=lc: e.copy(
                    kvlT_s[:, lc * 512:(lc + 1) * 512], pX[:, :]),
                    waits=[("sT", vmm)] + tk["kvlT"].war(), inc=("sC", 1))
                tk["pX"].read("sC", vev)
                tk["kvlT"].wrote("sC", vev)

        def a1_c(g, t):
            for lc in range(4):
                vmm = E("tensor", lambda e, t=t, lc=lc: e.matmul(
                    pX[:, 0:DC], kvlT_s[:, lc * 512 + t * 128: lc * 512 + t * 128 + 128],
                    wcT_s[:, lc * DC:(lc + 1) * DC],
                    start=(lc == 0), stop=(lc == 3)),
                    waits=(tk["kvlT"].raw() + tk["pX"].war() +
                           ([wcT_done] if g == 0 else [])),
                    inc=("sT", 1))
            tk["pX"].wrote("sT", vmm)
            vev = E("scalar", lambda e: e.copy(c_f32[:, :], pX[:, 0:DC]),
                    waits=[("sT", vmm)] + tk["c_f32"].war(), inc=("sC", 1))
            tk["pX"].read("sC", vev)
            tk["c_f32"].wrote("sC", vev)
            vln = layer_norm(c_f32[:, :], qb_s[:, 0:DC], tk["qb"],
                             [("sC", vev)], cnbf_s[:, :], tk["cnbf"], 16 + (t % 2) * 8)
            tk["c_f32"].read("sV", vln)
            for cc in range(2):
                sem, vv = pbf_transpose(
                    cnbf_s[:, cc * 128:(cc + 1) * 128],
                    cnT_s[:, cc * 512 + t * 128: cc * 512 + t * 128 + 128],
                    (t * 2 + cc) % 8, tk["cnbf"].raw() + tk["cnT"].war(),
                    evac_eng="vector")
                tk["cnbf"].read("sT", p.cnt["sT"])
                tk["cnT"].wrote(sem, vv)

        def a1_pieces(g):
            ps = [lambda t=t: a1_tile(g, t) for t in range(4)]
            ps.append(lambda: a1_kvlat(g))
            ps += [lambda t=t: a1_c(g, t) for t in range(4)]
            return ps

        def a1_qkv_gen(g):
            vcs = p.dma("sync", cosg_s[:, :], cosT_d[:, g * 512:(g + 1) * 512],
                        waits=tk["cosg"].war())
            tk["cosg"].wrote(*vcs)
            vsn = p.dma("sync", sing_s[:, :], sinT_d[:, g * 512:(g + 1) * 512],
                        waits=tk["sing"].war())
            tk["sing"].wrote(*vsn)
            for qc in range(2):
                pb = (pS0, "pS0") if qc % 2 == 0 else (pS1, "pS1")
                for cc in range(2):
                    vmm = E("tensor", lambda e, qc=qc, cc=cc, pb=pb: e.matmul(
                        pb[0][:, :], wqT_s[:, cc * DC + qc * 128: cc * DC + qc * 128 + 128],
                        cnT_s[:, cc * 512:(cc + 1) * 512],
                        start=(cc == 0), stop=(cc == 1)),
                        waits=tk["cnT"].raw() + tk[pb[1]].war() + (tk["wqT"].raw() if g == 0 else []),
                        inc=("sT", 1))
                tk[pb[1]].wrote("sT", vmm)
                vev = E("vector", lambda e, qc=qc, pb=pb: e.tensor_scalar(
                    out=qb_s[:, qc * 512:(qc + 1) * 512], in0=pb[0][:, :],
                    scalar1=bq_s[:, qc:qc + 1], scalar2=None, op0=ALU.add),
                    waits=[("sT", vmm), ("sC", BE[0])] + tk["qb"].war(), inc=("sV", 1))
                tk[pb[1]].read("sV", vev)
                tk["qb"].wrote("sV", vev)
            cosg = cosg_s[:, :]
            sing = sing_s[:, :]
            v1 = E("vector", lambda e: e.tensor_tensor(
                out=ropet0[:, :], in0=qb_s[:, 0:512], in1=cosg, op=ALU.mult),
                waits=tk["qb"].raw() + tk["rt0"].war() + [vcs, vsn], inc=("sV", 1))
            tk["rt0"].wrote("sV", v1)
            v2 = E("vector", lambda e: e.tensor_tensor(
                out=ropet1[:, :], in0=qb_s[:, 512:1024], in1=sing, op=ALU.mult),
                waits=tk["qb"].raw() + tk["rt1"].war(), inc=("sV", 1))
            tk["rt1"].wrote("sV", v2)
            v3 = E("vector", lambda e: e.tensor_tensor(
                out=qT_s[:, 0:512], in0=ropet0[:, :], in1=ropet1[:, :], op=ALU.subtract),
                waits=[("sV", v2)] + tk["qT"].war(), inc=("sV", 1))
            v4 = E("vector", lambda e: e.tensor_tensor(
                out=ropet0[:, :], in0=qb_s[:, 512:1024], in1=cosg, op=ALU.mult),
                waits=[("sV", v3)], inc=("sV", 1))
            v5 = E("vector", lambda e: e.tensor_tensor(
                out=ropet1[:, :], in0=qb_s[:, 0:512], in1=sing, op=ALU.mult),
                waits=[("sV", v4)], inc=("sV", 1))
            v6 = E("vector", lambda e: e.tensor_tensor(
                out=qT_s[:, 512:1024], in0=ropet0[:, :], in1=ropet1[:, :], op=ALU.add),
                waits=[("sV", v5)], inc=("sV", 1))
            tk["qT"].wrote("sV", v6)
            tk["qb"].read("sV", v5)
            tk["rt0"].wrote("sV", v4)
            tk["rt1"].wrote("sV", v5)
            yield
            for kc in range(2):
                pb = (pS0, "pS0") if kc % 2 == 0 else (pS1, "pS1")
                for cc in range(2):
                    vmm = E("tensor", lambda e, kc=kc, cc=cc, pb=pb: e.matmul(
                        pb[0][:, :], wkvT_s[:, cc * 2 * DC + kc * 128: cc * 2 * DC + kc * 128 + 128],
                        cnT_s[:, cc * 512:(cc + 1) * 512],
                        start=(cc == 0), stop=(cc == 1)),
                        waits=tk["cnT"].raw() + tk[pb[1]].war() + (tk["wkvT"].raw() if g == 0 else []),
                        inc=("sT", 1))
                tk[pb[1]].wrote("sT", vmm)
                vev = E("vector", lambda e, kc=kc, pb=pb: e.tensor_scalar(
                    out=qb_s[:, kc * 512:(kc + 1) * 512], in0=pb[0][:, :],
                    scalar1=bkv_s[:, kc:kc + 1], scalar2=None, op0=ALU.add),
                    waits=[("sT", vmm), ("sC", BE[0])] + tk["qb"].war(), inc=("sV", 1))
                tk[pb[1]].read("sV", vev)
                tk["qb"].wrote("sV", vev)
            kdst0 = KT_s[:, 0 * T + g * 512: 0 * T + g * 512 + 512]
            kdst1 = KT_s[:, 1 * T + g * 512: 1 * T + g * 512 + 512]
            v1 = E("vector", lambda e: e.tensor_tensor(
                out=ropet0[:, :], in0=qb_s[:, 0:512], in1=cosg, op=ALU.mult),
                waits=tk["qb"].raw() + tk["rt0"].war(), inc=("sV", 1))
            v2 = E("vector", lambda e: e.tensor_tensor(
                out=ropet1[:, :], in0=qb_s[:, 512:1024], in1=sing, op=ALU.mult),
                waits=tk["qb"].raw() + tk["rt1"].war(), inc=("sV", 1))
            v3 = E("vector", lambda e, kdst0=kdst0: e.tensor_tensor(
                out=kdst0, in0=ropet0[:, :], in1=ropet1[:, :], op=ALU.subtract),
                waits=[("sV", v2)] + tk["KT"].war(), inc=("sV", 1))
            v4 = E("vector", lambda e: e.tensor_tensor(
                out=ropet0[:, :], in0=qb_s[:, 512:1024], in1=cosg, op=ALU.mult),
                waits=[("sV", v3)], inc=("sV", 1))
            v5 = E("vector", lambda e: e.tensor_tensor(
                out=ropet1[:, :], in0=qb_s[:, 0:512], in1=sing, op=ALU.mult),
                waits=[("sV", v4)], inc=("sV", 1))
            v6 = E("vector", lambda e, kdst1=kdst1: e.tensor_tensor(
                out=kdst1, in0=ropet0[:, :], in1=ropet1[:, :], op=ALU.add),
                waits=[("sV", v5)], inc=("sV", 1))
            tk["KT"].wrote("sV", v6)
            tk["qb"].read("sV", v5)
            tk["rt0"].wrote("sV", v4)
            tk["rt1"].wrote("sV", v5)
            tk["cosg"].read("sV", v6)
            tk["sing"].read("sV", v6)
            yield
            for kc in (2, 3):
                pb = (pS0, "pS0") if kc % 2 == 0 else (pS1, "pS1")
                for cc in range(2):
                    vmm = E("tensor", lambda e, kc=kc, cc=cc, pb=pb: e.matmul(
                        pb[0][:, :], wkvT_s[:, cc * 2 * DC + kc * 128: cc * 2 * DC + kc * 128 + 128],
                        cnT_s[:, cc * 512:(cc + 1) * 512],
                        start=(cc == 0), stop=(cc == 1)),
                        waits=tk["cnT"].raw() + tk[pb[1]].war() + (tk["wkvT"].raw() if g == 0 else []),
                        inc=("sT", 1))
                tk[pb[1]].wrote("sT", vmm)
                vev = E("vector", lambda e, kc=kc, pb=pb: e.tensor_scalar(
                    out=vbf_s[:, :], in0=pb[0][:, :],
                    scalar1=bkv_s[:, kc:kc + 1], scalar2=None, op0=ALU.add),
                    waits=[("sT", vmm), ("sC", BE[0])] + tk["vbf"].war(), inc=("sV", 1))
                tk[pb[1]].read("sV", vev)
                tk["vbf"].wrote("sV", vev)
                dvc = kc - 2
                for t in range(4):
                    it = g * 4 + t
                    sem, vv = pbf_transpose(
                        vbf_s[:, t * 128:(t + 1) * 128],
                        V_s[:, it * VST + dvc * 128: it * VST + dvc * 128 + 128],
                        (4 + t) % 8, tk["vbf"].raw() + tk["V"].war(), evac_eng="vector")
                    tk["vbf"].read("sT", p.cnt["sT"])
                    tk["V"].wrote(sem, vv)
                yield

        def emit_scores_exp(g, kb):
            b = kb % 2
            ps, psn = (pS0, "pS0") if kb % 2 == 0 else (pS1, "pS1")
            for cc in range(2):
                vmm = E("tensor", lambda e, kb=kb, cc=cc, ps=ps: e.matmul(
                    ps[:, :], KT_s[:, cc * T + kb * 128: cc * T + kb * 128 + 128],
                    qT_s[:, cc * 512:(cc + 1) * 512],
                    start=(cc == 0), stop=(cc == 1)),
                    waits=tk["qT"].raw() + tk["KT"].raw() + tk[psn].war(),
                    inc=("sT", 1))
            tk[psn].wrote("sT", vmm)
            cut = max(0, kb - g * 4) * 128
            if kb >= g * 4:
                t0 = kb - g * 4
                vm = E("vector", lambda e, t0=t0, ps=ps: e.tensor_tensor(
                    out=scores_s[:, 0:128], in0=ps[:, t0 * 128:(t0 + 1) * 128],
                    in1=mask_s[:, :], op=ALU.add),
                    waits=[("sT", vmm)] + tk["scores"].war(), inc=("sV", 1))
                tk[psn].read("sV", vm)
                tk["scores"].wrote("sV", vm)
                vex = E("scalar", lambda e, b=b, t0=t0: e.activation(
                    out=expb[b][:, t0 * 128:(t0 + 1) * 128], in_=scores_s[:, 0:128],
                    func=AF.Exp, scale=SCALE),
                    waits=[("sV", vm)] + tk[f"expb{b}"].war(), inc=("sC", 1))
                tk["scores"].read("sC", vex)
                tk[f"expb{b}"].wrote("sC", vex)
                if t0 < 3:
                    vex = E("scalar", lambda e, b=b, t0=t0, ps=ps: e.activation(
                        out=expb[b][:, (t0 + 1) * 128:512], in_=ps[:, (t0 + 1) * 128:512],
                        func=AF.Exp, scale=SCALE),
                        waits=[("sT", vmm)], inc=("sC", 1))
                    tk[psn].read("sC", vex)
                    tk[f"expb{b}"].wrote("sC", vex)
            else:
                vex = E("scalar", lambda e, b=b, ps=ps: e.activation(
                    out=expb[b][:, :], in_=ps[:, :], func=AF.Exp, scale=SCALE),
                    waits=[("sT", vmm)] + tk[f"expb{b}"].war(), inc=("sC", 1))
                tk[psn].read("sC", vex)
                tk[f"expb{b}"].wrote("sC", vex)
            return kb, vex, cut

        def emit_avs(g, st):
            kb, vex, cut = st
            b = kb % 2
            for t in range(cut // 128, 4):
                it = g * 4 + t
                vav = E("tensor", lambda e, b=b, t=t, kb=kb: e.matmul(
                    pO[:, t * 512: t * 512 + DC + 1], expb[b][:, t * 128:(t + 1) * 128],
                    V_s[:, kb * VST: kb * VST + DC + 1],
                    start=(kb == 0), stop=(kb == it), skip_group_check=True),
                    waits=[("sC", vex)] + tk["V"].raw() +
                          (tkpo[t].war() if kb == 0 else []),
                    inc=("sT", 1))
                tk[f"expb{b}"].read("sT", vav)
                if kb == it:
                    tkpo[t].wrote("sT", vav)
                    vrc = E("vector", lambda e, t=t: e.reciprocal(
                        stat_s[:, 48 + t:49 + t], pO[:, t * 512 + DC: t * 512 + DC + 1]),
                        waits=[("sT", vav)] + stt(48 + t).war(), inc=("sV", 1))
                    tkpo[t].read("sV", vrc)
                    stt(48 + t).wrote("sV", vrc)
                    vo = E("scalar", lambda e, t=t: e.activation(
                        out=o_s[:, t * DC:(t + 1) * DC], in_=pO[:, t * 512: t * 512 + DC],
                        func=AF.Copy, scale=stat_s[:, 48 + t:49 + t]),
                        waits=[("sV", vrc)] + tk["o"].war(), inc=("sC", 1))
                    tkpo[t].read("sC", vo)
                    tk["o"].wrote("sC", vo)
                    stt(48 + t).read("sC", vo)

        def outproj_tile(g, t):
            it = g * 4 + t
            xb, xbn = xa_b[t % 2], f"xa{t % 2}"
            for nq in range(4):
                for lc in range(4):
                    vmm = E("tensor", lambda e, t=t, nq=nq, lc=lc: e.matmul(
                        pO[:, nq * 512:(nq + 1) * 512],
                        zT_s[:, lc * 512 + t * 128: lc * 512 + t * 128 + 128],
                        wcomb_s[:, lc * D + nq * 512: lc * D + nq * 512 + 512],
                        start=(lc == 0), stop=False),
                        waits=(tk["zT"].raw() +
                               (tkpo[nq].war() if lc == 0 else []) +
                               ([("sV", wcomb_done)] if g == 0 else [])),
                        inc=("sT", 1))
                vmm = E("tensor", lambda e, nq=nq: e.matmul(
                    pO[:, nq * 512:(nq + 1) * 512], onesb_s[0:1, 0:128],
                    borbf_s[0:1, nq * 512:(nq + 1) * 512],
                    start=False, stop=True),
                    waits=([v_onesb] + tk["borbf"].raw()) if g == 0 else [],
                    inc=("sT", 1))
                tkpo[nq].wrote("sT", vmm)
            vx = p.dma("sync", xb[:, :], x_d[it * 128:(it + 1) * 128, :],
                       waits=tk[xbn].war())
            tk[xbn].wrote(*vx)
            vadd = E("vector", lambda e, xb=xb: e.tensor_tensor(
                out=xmid_s[:, :], in0=pO[:, :], in1=xb[:, :], op=ALU.add),
                waits=[("sT", vmm), vx] + tk["xmid"].war(), inc=("sV", 1))
            for trk in tkpo:
                trk.read("sV", vadd)
            tk[xbn].read("sV", vadd)
            tk["xmid"].wrote("sV", vadd)
            vln = layer_norm(xmid_s[:, :], scores_s[:, 0:D], tk["scores"],
                             [("sV", vadd)], xlnbf_s[:, :], tk["xlnbf"], 32 + (t % 2) * 8)
            tk["xmid"].read("sV", vln)
            vst1 = p.dma("gpsimd", xmid_d[it * 128:(it + 1) * 128, :], xmid_s[:, :],
                         waits=[("sV", vln)])
            tk["xmid"].read(*vst1)
            xmid_store[it] = vst1
            vst2 = p.dma("gpsimd", xln2_d[it * 128:(it + 1) * 128, :], xlnbf_s[:, :],
                         waits=[("sV", vln)])
            tk["xlnbf"].read(*vst2)
            xln2_store[it] = vst2
            cslot()

        for f in a1_pieces(0):
            f()
        bias_prep()
        for _ in a1_qkv_gen(0):
            pass
        for g in range(NG):
            nxt = a1_pieces(g + 1) if g + 1 < NG else []
            ni = [0]

            def next_a1(nxt=nxt, ni=ni):
                if ni[0] < len(nxt):
                    nxt[ni[0]]()
                    ni[0] += 1

            nkb = g * 4 + 4
            npc = len(nxt)
            av_pending = None
            for kb in range(nkb):
                st_kb = emit_scores_exp(g, kb)
                if av_pending is not None:
                    emit_avs(g, av_pending)
                av_pending = st_kb
                while ni[0] < npc and ni[0] * nkb <= kb * npc:
                    next_a1()
            emit_avs(g, av_pending)
            for t in range(4):
                for dvc in range(2):
                    sem, vv = pbf_transpose(
                        o_s[:, t * DC + dvc * 128: t * DC + dvc * 128 + 128],
                        oT_s[:, dvc * 512 + t * 128: dvc * 512 + t * 128 + 128],
                        (t * 2 + dvc) % 8, tk["o"].raw() + tk["oT"].war(), evac_eng="vector")
                    tk["o"].read("sT", p.cnt["sT"])
                    tk["oT"].wrote(sem, vv)
            next_a1()
            for lc in range(4):
                pb = (pS0, "pS0") if lc % 2 == 0 else (pS1, "pS1")
                for dvc in range(2):
                    vmm = E("tensor", lambda e, lc=lc, dvc=dvc, pb=pb: e.matmul(
                        pb[0][:, :], wc2lT_s[:, dvc * DL + lc * 128: dvc * DL + lc * 128 + 128],
                        oT_s[:, dvc * 512:(dvc + 1) * 512],
                        start=(dvc == 0), stop=(dvc == 1)),
                        waits=(tk["oT"].raw() + tk[pb[1]].war() +
                               ([wc2lT_done] if g == 0 else [])),
                        inc=("sT", 1))
                tk[pb[1]].wrote("sT", vmm)
                vev = E("scalar", lambda e, lc=lc, pb=pb: e.copy(
                    zT_s[:, lc * 512:(lc + 1) * 512], pb[0][:, :]),
                    waits=[("sT", vmm)] + tk["zT"].war(), inc=("sC", 1))
                tk[pb[1]].read("sC", vev)
                tk["zT"].wrote("sC", vev)
            next_a1()
            while ni[0] < npc:
                next_a1()
            qg = a1_qkv_gen(g + 1) if g + 1 < NG else None
            for t in range(4):
                outproj_tile(g, t)
                if qg is not None:
                    next(qg, None)
            if qg is not None:
                for _ in qg:
                    pass
        run_q(DQa, dqa_i, 10 ** 9)
        run_q(DQb, dqb_i, 10 ** 9)
        run_q(CQ, cq_i, 10 ** 9)
    w13t_all = p.dma_all_done("vdma") + p.dma_all_done("gpsimd")
    w2p_all = p.dma_all_done("vdma") + p.dma_all_done("gpsimd")

    st_A2.close()
    st_psA.close()
    st_A.close()

    A_end = [(s, v) for s, v in p.cnt.items() if v]

    if phase_b:
        st_B = ExitStack()
        xln2T_s = st_B.enter_context(nc.sbuf_tensor("xln2T_s", [128, 16 * C], BF16))
        oacc = [st_B.enter_context(nc.sbuf_tensor(f"oacc{i}", [128, D], F32)) for i in range(C // 128)]
        hid_s = st_B.enter_context(nc.sbuf_tensor("hid_s", [128, (HG // 128) * C], BF16))
        w1sl = [st_B.enter_context(nc.sbuf_tensor(f"w1sl{i}", [128, 16 * 256], BF16)) for i in range(2)]
        w3sl = [st_B.enter_context(nc.sbuf_tensor(f"w3sl{i}", [128, 16 * 256], BF16)) for i in range(2)]
        w2g_s = st_B.enter_context(nc.sbuf_tensor("w2g_s", [128, (HG // 128) * D], BF16))
        sg_s = st_B.enter_context(nc.sbuf_tensor("sg_s", [128, C], F32))
        xbm = [st_B.enter_context(nc.sbuf_tensor(f"xbm{i}", [128, D], F32)) for i in range(2)]

        pG = st_B.enter_context(nc.psum_tensor("pG", [128, C], F32))
        pL = st_B.enter_context(nc.psum_tensor("pL", [128, C], F32))
        pDa = st_B.enter_context(nc.psum_tensor("pDa", [128, 1024], F32))
        pDb = st_B.enter_context(nc.psum_tensor("pDb", [128, 1024], F32))
        for nm in ("xln2T", "w2g", "sg", "xbm", "xbm2", "pG", "pL", "pDa", "pDb"):
            for s, v in A_end:
                tk[nm].read(s, v)

        w1t_r = w1t.rearrange("(dc p) h -> p dc h", p=128)
        w3t_r = w3t.rearrange("(dc p) h -> p dc h", p=128)
        w2p_r2 = w2p.rearrange("(hc p) d -> p hc d", p=128)

        tkoa = [Trk() for _ in range(C // 128)]
        tkhid = [Trk() for _ in range(HG // 128)]
        w1trk = [Trk() for _ in range(2)]
        w3trk = [Trk() for _ in range(2)]
        for trk in tkoa + tkhid + w1trk + w3trk:
            for s, v in A_end:
                trk.read(s, v)

        def load_xln2T(sw):
            c0 = sw * C
            gate = [xln2_store[c0 // 128 + k] for k in range(C // 128)] if phase_a else []
            for dc in range(16):
                vv = p.dma("sync", xln2T_s[:, dc * C:(dc + 1) * C],
                           xln2_d[c0:c0 + C, dc * 128:(dc + 1) * 128],
                           waits=gate + tk["xln2T"].war(), transpose=True)
                tk["xln2T"].wrote(*vv)
            return vv

        xln2T_pending = load_xln2T(0)
        for sw in range(NSW):
            c0 = sw * C
            xmid_tok = [None] * (C // 128)
            xln2T_ready = xln2T_pending
            if debug and sw == 0:
                for dc in range(16):
                    vdbg = p.dma("gpsimd", dbgXT[dc * 128:(dc + 1) * 128, 0:C],
                                 xln2T_s[:, dc * C:(dc + 1) * C], waits=[xln2T_ready])
                    tk["xln2T"].read(*vdbg)

            NHG = HID // HG
            for hg in range(NHG):
                h0 = hg * HG
                NSL = HG // 256
                slice_ready = [None] * NSL

                def load_slice(sl4, first=False):
                    b = sl4 % 2
                    hh = h0 + sl4 * 256
                    va = p.dma("sync", w1sl[b][:, :], w1t_r[:, :, hh:hh + 256],
                               waits=(w13t_all if first else []) + w1trk[b].war())
                    w1trk[b].wrote(*va)
                    vb = p.dma("sync", w3sl[b][:, :], w3t_r[:, :, hh:hh + 256],
                               waits=w3trk[b].war())
                    w3trk[b].wrote(*vb)
                    slice_ready[sl4] = (va, vb)
                    if debug and sw == 0 and hg == 0 and sl4 == 0:
                        vdbg = p.dma("gpsimd", dbgW1[:, :], w1sl[b][:, :], waits=[va])
                        w1trk[b].read(*vdbg)

                load_slice(0, first=(sw == 0 and hg == 0))
                if NSL > 1:
                    load_slice(1)
                vw2 = p.dma("sync", w2g_s[:, :], w2p_r2[:, h0 // 128:(h0 + HG) // 128, :],
                            waits=(w2p_all if (sw == 0 and hg == 0) else []) + tk["w2g"].war())
                tk["w2g"].wrote(*vw2)
                for hc in range(HG // 128):
                    b = (hc // 2) % 2
                    va, vb = slice_ready[hc // 2]
                    col = (hc % 2) * 128
                    for half in range(C // 512):
                        for dc in range(16):
                            vg = E("tensor", lambda e, b=b, col=col, dc=dc, half=half: e.matmul(
                                pG[:, half * 512:(half + 1) * 512],
                                w1sl[b][:, dc * 256 + col: dc * 256 + col + 128],
                                xln2T_s[:, dc * C + half * 512: dc * C + half * 512 + 512],
                                start=(dc == 0), stop=(dc == 15)),
                                waits=([va, xln2T_ready] + tk["pG"].war()) if (dc == 0 and half == 0) else [],
                                inc=("sT", 1))
                    tk["pG"].wrote("sT", vg)
                    w1trk[b].read("sT", vg)
                    for half in range(C // 512):
                        for dc in range(16):
                            vl = E("tensor", lambda e, b=b, col=col, dc=dc, half=half: e.matmul(
                                pL[:, half * 512:(half + 1) * 512],
                                w3sl[b][:, dc * 256 + col: dc * 256 + col + 128],
                                xln2T_s[:, dc * C + half * 512: dc * C + half * 512 + 512],
                                start=(dc == 0), stop=(dc == 15)),
                                waits=([vb] + tk["pL"].war()) if (dc == 0 and half == 0) else [],
                                inc=("sT", 1))
                    tk["pL"].wrote("sT", vl)
                    w3trk[b].read("sT", vl)
                    tk["xln2T"].read("sT", vl)
                    if hc % 2 == 1 and hc // 2 + 2 < NSL:
                        load_slice(hc // 2 + 2)
                    vsi = E("scalar", lambda e: e.activation(
                        out=sg_s[:, :], in_=pG[:, :], func=AF.Silu),
                        waits=[("sT", vg)] + tk["sg"].war(), inc=("sC", 1))
                    tk["pG"].read("sC", vsi)
                    tk["sg"].wrote("sC", vsi)
                    hslot = hc
                    vmu = E("vector", lambda e, hslot=hslot: e.tensor_tensor(
                        out=hid_s[:, hslot * C:(hslot + 1) * C], in0=sg_s[:, :],
                        in1=pL[:, :], op=ALU.mult),
                        waits=[("sC", vsi), ("sT", vl)] + tkhid[hslot].war(), inc=("sV", 1))
                    tk["pL"].read("sV", vmu)
                    tk["sg"].read("sV", vmu)
                    tkhid[hslot].wrote("sV", vmu)
                    if debug and sw == 0 and hg == 0:
                        vdbg = p.dma("gpsimd", dbgH[hslot * 128:(hslot + 1) * 128, 0:C],
                                     hid_s[:, hslot * C:(hslot + 1) * C],
                                     waits=[("sV", vmu)])
                        tkhid[hslot].read(*vdbg)
                if hg == NHG - 1 and sw + 1 < NSW:
                    xln2T_pending = load_xln2T(sw + 1)

                def load_xmid(tt):
                    it = (c0 // 128) + tt
                    bn = "xbm" if tt % 2 == 0 else "xbm2"
                    vxm = p.dma("sync", xbm[tt % 2][:, :], xmid_d[it * 128:(it + 1) * 128, :],
                                waits=([xmid_store[it]] if phase_a else []) + tk[bn].war())
                    tk[bn].wrote(*vxm)
                    xmid_tok[tt] = vxm

                if hg == 0:
                    load_xmid(0)
                    load_xmid(1)
                hid_ready = [tkhid[i].raw() for i in range(HG // 128)]
                for tt in range(C // 128):
                    if hg == 0 and 1 <= tt and tt + 1 < C // 128:
                        load_xmid(tt + 1)
                    for half in range(2):
                        pD = (pDa, "pDa") if (tt * 2 + half) % 2 == 0 else (pDb, "pDb")
                        for hc in range(HG // 128):
                            for n2 in range(2):
                                nd0 = half * 1024 + n2 * 512
                                vdm = E("tensor", lambda e, tt=tt, hc=hc, n2=n2, nd0=nd0, pD=pD: e.matmul(
                                    pD[0][:, n2 * 512:(n2 + 1) * 512],
                                    hid_s[:, hc * C + tt * 128: hc * C + tt * 128 + 128],
                                    w2g_s[:, hc * D + nd0: hc * D + nd0 + 512],
                                    start=(hc == 0), stop=(hc == HG // 128 - 1)),
                                    waits=(hid_ready[hc] + [vw2] + tk[pD[1]].war()) if (hc == 0 and n2 == 0) else (hid_ready[hc] if n2 == 0 else []),
                                    inc=("sT", 1))
                        tk[pD[1]].wrote("sT", vdm)
                        for hc in range(HG // 128):
                            tkhid[hc].read("sT", vdm)
                        tk["w2g"].read("sT", vdm)
                        if hg == 0:
                            bn = "xbm" if tt % 2 == 0 else "xbm2"
                            vea = E("vector", lambda e, tt=tt, half=half, pD=pD: e.tensor_tensor(
                                out=oacc[tt][:, half * 1024:(half + 1) * 1024],
                                in0=pD[0][:, :], in1=xbm[tt % 2][:, half * 1024:(half + 1) * 1024],
                                op=ALU.add),
                                waits=[("sT", vdm), xmid_tok[tt]] + tkoa[tt].war(), inc=("sV", 1))
                            tk[bn].read("sV", vea)
                        else:
                            vea = E("vector", lambda e, tt=tt, half=half, pD=pD: e.tensor_tensor(
                                out=oacc[tt][:, half * 1024:(half + 1) * 1024],
                                in0=pD[0][:, :], in1=oacc[tt][:, half * 1024:(half + 1) * 1024],
                                op=ALU.add),
                                waits=[("sT", vdm)] + tkoa[tt].war(), inc=("sV", 1))
                        tk[pD[1]].read("sV", vea)
                        tkoa[tt].wrote("sV", vea)
                        if hg == NHG - 1 and half == 1:
                            it = (c0 // 128) + tt
                            vso = p.dma("gpsimd", out_d[it * 128:(it + 1) * 128, :],
                                        oacc[tt][:, :], waits=[("sV", vea)])
                            tkoa[tt].read(*vso)
                        if debug and sw == 0 and hg == 0:
                            vdbg = p.dma("gpsimd",
                                         dbgO[tt * 128:(tt + 1) * 128, half * 1024:(half + 1) * 1024],
                                         oacc[tt][:, half * 1024:(half + 1) * 1024],
                                         waits=[("sV", vea)])
                            tkoa[tt].read(*vdbg)
        st_B.close()

    E("gpsimd", None, waits=p.dma_all_done("gpsimd"))
    E("sync", None, waits=p.dma_all_done("vdma"))
    E("sync", None, waits=p.dma_all_done("sync") + p.dma_all_done("s2"))

    with nc.Block() as block:
        block.sync(lambda e: [f(e) for f in p.ops["sync"]])
        block.scalar(lambda e: [f(e) for f in p.ops["scalar"]])
        block.vector(lambda e: [f(e) for f in p.ops["vector"]])
        block.tensor(lambda e: [f(e) for f in p.ops["tensor"]])
        block.gpsimd(lambda e: [f(e) for f in p.ops["gpsimd"]])

    if phase_b:
        st_B.close()
    return nc


def kernel(**inputs):
    def f32(a):
        return np.ascontiguousarray(np.asarray(a, dtype=np.float32))

    x = f32(inputs["L0_attn_x"])
    cosT, sinT = host_consts()
    common = {
        "wtkv": f32(inputs["L0_attn_W_to_kv"]),
        "wc": f32(inputs["L0_attn_W_compress"]),
        "gam": f32(inputs["L0_attn_ln_gamma"]),
        "bet": f32(inputs["L0_attn_ln_beta"]),
        "wq": f32(inputs["L0_attn_W_q_comp"]),
        "wkv": f32(inputs["L0_attn_W_kv_comp"]),
        "wc2l": f32(inputs["L0_attn_W_comp2lat"]),
        "wor": f32(inputs["L0_attn_W_out_res"]),
        "bor": f32(inputs["L0_attn_b_out_res"]),
        "w1": f32(inputs["L0_ffn_W1"]),
        "w2": f32(inputs["L0_ffn_W2"]),
        "w3": f32(inputs["L0_ffn_W3"]),
        "cosT": cosT,
        "sinT": sinT,
    }
    set_dims()
    nc = build()
    in_maps = [dict(common, x=x[c]) for c in range(8)]
    res = run_bass_kernel_spmd(nc, in_maps, core_ids=list(range(8)))
    return np.stack([r["out"] for r in res.results], axis=0)


if __name__ == "__main__":
    nc = build()
    print("instructions:", nc.n_instructions())
